# version 12
# baseline (speedup 1.0000x reference)
import math
import os
from contextlib import ExitStack

_PHASE_LIMIT = int(os.environ.get("KPHASE", "99"))

import numpy as np

import concourse.bass as bass
import concourse.mybir as mybir
from concourse.alu_op_type import AluOpType as AL
from concourse.bass_utils import run_bass_kernel_spmd

F32 = mybir.dt.float32
BF16 = mybir.dt.bfloat16
I32 = mybir.dt.int32
AF = mybir.ActivationFunctionType
AX = mybir.AxisListType.X

C = 1024
T = 2048
TG = 16384
NCC = 8
NCHG = TG // 128
NCHL = T // 128
HD = 64
E = 8
IEXP = 512
LAMBDA_INIT = 0.8 - 0.6 * math.exp(-0.3 * 4)
ONE_MINUS_LI = 1.0 - LAMBDA_INIT
RMS_EPS = float(np.finfo(np.float32).eps)
LN_EPS = 1e-5
HUGE = 1.0e15
INVC = 1.0 / C
INV2PI = 1.0 / (2.0 * math.pi)
TWOPI = 2.0 * math.pi


class Ctr:

    def __init__(self, handle):
        self.h = handle
        self.n = 0

    def inc(self, instr, k=1):
        instr.then_inc(self.h, k)
        self.n += k
        return self.n


class DmaStream:

    def __init__(self, nc, ctx, name, depth):
        self.depth = depth
        self.sems = [Ctr(ctx.enter_context(nc.semaphore(f"{name}{i}")))
                     for i in range(depth)]
        self.i = 0

    def issue(self, eng, out_ap, in_ap, **kw):
        tok = self.i
        sc = self.sems[tok % self.depth]
        if tok >= self.depth:
            eng.wait_ge(sc.h, 16 * (tok // self.depth))
        eng.dma_start(out_ap, in_ap, **kw).then_inc(sc.h, 16)
        sc.n += 16
        self.i += 1
        return tok

    def wait(self, eng, tok):
        sc = self.sems[tok % self.depth]
        eng.wait_ge(sc.h, 16 * (tok // self.depth + 1))

    def wait_all(self, eng):
        for sc in self.sems:
            eng.wait_ge(sc.h, sc.n)


def build_program():
    nc = bass.Bass("TRN2", target_bir_lowering=False)

    xT = nc.dram_tensor("xT", [C, TG], F32, kind="ExternalInput").ap()
    wqT = nc.dram_tensor("wqT", [C, C], F32, kind="ExternalInput").ap()
    wkT = nc.dram_tensor("wkT", [C, C], F32, kind="ExternalInput").ap()
    wvT = nc.dram_tensor("wvT", [C, C], F32, kind="ExternalInput").ap()
    wpT = nc.dram_tensor("wpT", [C, C], F32, kind="ExternalInput").ap()
    gwTd = nc.dram_tensor("gwTd", [C, E], F32, kind="ExternalInput").ap()
    lbd = nc.dram_tensor("lbd", [1, E], F32, kind="ExternalInput").ap()
    egwT = nc.dram_tensor("egwT", [E, C, IEXP], F32, kind="ExternalInput").ap()
    euwT = nc.dram_tensor("euwT", [E, C, IEXP], F32, kind="ExternalInput").ap()
    edwT = nc.dram_tensor("edwT", [E, IEXP, C], F32, kind="ExternalInput").ap()
    sgwT = nc.dram_tensor("sgwT", [C, 2 * IEXP], F32, kind="ExternalInput").ap()
    suwT = nc.dram_tensor("suwT", [C, 2 * IEXP], F32, kind="ExternalInput").ap()
    sdwT = nc.dram_tensor("sdwT", [2 * IEXP, C], F32, kind="ExternalInput").ap()
    lqk_d = nc.dram_tensor("lqk_d", [1, 4 * HD], F32, kind="ExternalInput").ap()
    cosd = nc.dram_tensor("cosd", [128, 16 * HD], F32, kind="ExternalInput").ap()
    sind = nc.dram_tensor("sind", [128, 16 * HD], F32, kind="ExternalInput").ap()
    out = nc.dram_tensor("out", [C, T], F32, kind="ExternalOutput").ap()
    dbg = nc.dram_tensor("dbg", [128, T], BF16,
                         kind="ExternalOutput" if os.environ.get("KDBG") else "Internal"
                         ).ap()
    otd = nc.dram_tensor("otd", [8, 2, 128, T], BF16,
                         kind="ExternalOutput" if os.environ.get("KDBG") else "Internal"
                         ).ap()

    ctx = ExitStack()

    def sb(name, shape, dt=F32, cx=None):
        return (cx or ctx).enter_context(nc.sbuf_tensor(name, shape, dt)).ap()

    def ps(name, shape, cx=None, dt=F32):
        return (cx or ctx).enter_context(nc.psum_tensor(name, shape, dt)).ap()

    LD = Ctr(ctx.enter_context(nc.semaphore("LD")))
    TE = Ctr(ctx.enter_context(nc.semaphore("TE")))
    VE = Ctr(ctx.enter_context(nc.semaphore("VE")))
    SE = Ctr(ctx.enter_context(nc.semaphore("SE")))
    ADS = DmaStream(nc, ctx, "AD", 4)
    SQS = DmaStream(nc, ctx, "SQ", 2)
    XTS = DmaStream(nc, ctx, "XT", 2)
    WFS = DmaStream(nc, ctx, "WF", 2)
    DDS = DmaStream(nc, ctx, "DD", 2)
    WAS = DmaStream(nc, ctx, "WA", 2)
    OTS = DmaStream(nc, ctx, "OT", 2)
    OBS = DmaStream(nc, ctx, "OB", 2)

    xnh = sb("xnh", [128, NCC, T], BF16)
    xnl = sb("xnl", [128, NCC, T], BF16)
    cc_all = sb("cc_all", [128, 16 * HD])
    ss_all = sb("ss_all", [128, 16 * HD])
    m0 = sb("m0", [128, 256])
    m1 = sb("m1", [128, 256])
    id128 = sb("id128", [128, 128])
    idb16 = sb("idb16", [128, 128], BF16)
    ones_row = sb("ones_row", [1, 128])
    ones_col = sb("ones_col", [128, 1])
    ones_colb = sb("ones_colb", [128, 1], BF16)
    e8t = sb("e8t", [128, 8])
    r8t = sb("r8t", [128, 8])
    lamneg = sb("lamneg", [128, 1])
    sc = sb("sc", [128, NCHL, E])
    gstat = sb("gstat", [128, NCC, 9], BF16)
    gw32 = sb("gw32", [128, NCC, E])
    cs8 = sb("cs8", [128, E])
    b8 = sb("b8", [128, E])
    wm8 = sb("wm8", [128, E])
    lqk = sb("lqk", [1, 4 * HD])
    lscr = sb("lscr", [1, 2 * HD])
    lsum = sb("lsum", [1, 2])
    lexp = sb("lexp", [1, 2])
    lamv = sb("lamv", [1, 1])
    cpool = sb("cpool", [128, 8])
    _CONSTS = [LN_EPS, 1.0]
    for k, val in enumerate(_CONSTS):
        nc.const_aps.register(F32, val, cpool[:, k:k + 1])

    sux = ExitStack()
    iota_f = sb("iota_f", [128, 256], cx=sux)
    iota_p = sb("iota_p", [128, 256], cx=sux)
    with nc.Block("su0") as blk:
        def f_g(g):
            g.iota(iota_f[:, :], [[1, 256]], channel_multiplier=0,
                   allow_small_or_imprecise_dtypes=True)
            g.iota(iota_p[:, :], [[0, 256]], channel_multiplier=1,
                   allow_small_or_imprecise_dtypes=True)
            g.iota(e8t[:, :], [[1, 8]], channel_multiplier=0,
                   allow_small_or_imprecise_dtypes=True)


        def f_sy(sy):
            sy.dma_start(gw32[:, :, :],
                         bass.AP(gwTd.tensor, 0, [[E, 128], [128 * E, NCC], [1, E]])
                         ).then_inc(LD.h, 16)
            LD.n += 16
            sy.dma_start(b8[0:1, :], lbd[:, :]).then_inc(LD.h, 16)
            LD.n += 16
            sy.dma_start(lqk[:, :], lqk_d[:, :]).then_inc(LD.h, 16)
            LD.n += 16
            sy.dma_start(cc_all[:, :], cosd[:, :]).then_inc(LD.h, 16)
            LD.n += 16
            sy.dma_start(ss_all[:, :], sind[:, :]).then_inc(LD.h, 16)
            LD.n += 16

        blk.gpsimd(f_g)
        blk.sync(f_sy)

    with nc.Block("su1") as blk:
        def f_v(v):
            v.wait_ge(LD.h, LD.n)
            v.tensor_tensor(m0[:, :], iota_p[:, :], iota_f[:, :], AL.is_le)
            v.tensor_scalar(m1[:, :], iota_p[:, :], 1.0, 128.0, AL.mult, AL.add)
            v.drain()
            v.tensor_tensor(m1[:, :], m1[:, :], iota_f[:, :], AL.is_le)
            v.tensor_tensor(id128[:, :], iota_f[:, 0:128], iota_p[:, 0:128],
                            AL.is_equal)
            v.drain()
            v.tensor_copy(idb16[:, :], id128[:, :])
            v.tensor_scalar(r8t[:, :], e8t[:, :], -1.0, 8.0, AL.mult, AL.add)
            v.memset(ones_row[:, :], 1.0)
            v.memset(ones_col[:, :], 1.0)
            v.memset(ones_colb[:, :], 1.0)
            for k, val in enumerate(_CONSTS):
                v.memset(cpool[:, k:k + 1], val)
            v.tensor_tensor(lscr[0:1, 0:HD], lqk[0:1, 0:HD],
                            lqk[0:1, HD:2 * HD], AL.mult)
            v.tensor_tensor(lscr[0:1, HD:2 * HD], lqk[0:1, 2 * HD:3 * HD],
                            lqk[0:1, 3 * HD:4 * HD], AL.mult)
            v.drain()
            v.reduce_sum(lsum[0:1, :].rearrange("a (b c) -> a b c", c=1),
                         lscr[0:1, :].rearrange("a (b c) -> a b c", b=2), axis=AX)

        blk.vector(f_v)

    with nc.Block("su2") as blk:
        def f_s(s):
            s.activation(lexp[0:1, :], lsum[0:1, :], AF.Exp)

        blk.scalar(f_s)

    with nc.Block("su3") as blk:
        def f_v(v):
            v.tensor_tensor(lamv[0:1, 0:1], lexp[0:1, 0:1], lexp[0:1, 1:2],
                            AL.subtract)
            v.drain()
            v.tensor_scalar(lamv[0:1, 0:1], lamv[0:1, 0:1], 1.0, LAMBDA_INIT,
                            AL.mult, AL.add)

        blk.vector(f_v)

    lam_ps = ps("lam_ps", [128, 512], cx=sux)
    with nc.Block("su4") as blk:
        def f_t(t):
            t.matmul(lam_ps[:, 0:1], ones_row[:, :], lamv[0:1, 0:1],
                     start=True, stop=True).then_inc(TE.h, 1)
            TE.n += 1

        def f_v(v):
            v.wait_ge(TE.h, TE.n)
            v.tensor_scalar(lamneg[:, :], lam_ps[:, 0:1], -1.0, None, AL.mult)

        blk.tensor(f_t)
        blk.vector(f_v)

    with nc.Block("su6") as blk:
        def f_s(s):
            s.copy(gstat[:, :, 0:8], gw32[:, :, :])

        blk.scalar(f_s)

    with nc.Block("su7") as blk:
        def f_v(v):
            v.memset(bass.AP(gstat.tensor, 8, [[NCC * 9, 128], [9, NCC], [1, 1]]),
                     1.0)

        blk.vector(f_v)

    sux.close()
    if _PHASE_LIMIT <= 1:
        ctx.close()
        return nc


    lnx = ExitStack()
    xloc = sb("xloc", [128, NCC, T], cx=lnx)
    xsq = sb("xsq", [128, 2, 512], cx=lnx)
    strow = sb("strow", [1, 3 * T], cx=lnx)
    MU, V2, MR = 0, 1, 2
    rbb = sb("rbb", [128, T], cx=lnx)
    mbb = sb("mbb", [128, T], cx=lnx)
    xscr = sb("xscr", [128, T], cx=lnx)
    lnp = ExitStack()
    psA = ps("psA", [1, T], cx=lnp)
    psB = ps("psB", [1, T], cx=lnp)

    with nc.Block("ln0") as blk:
        ld_b = LD.n

        def f_sy(sy):
            for cci in range(NCC):
                sy.dma_start(xloc[:, cci, :],
                             bass.AP(xT.tensor, cci * 128 * TG, [[TG, 128], [1, T]])
                             ).then_inc(LD.h, 16)
                LD.n += 16

        def f_g(g):
            ADS.issue(g, out[:, :], bass.AP(xT.tensor, 0, [[TG, C], [1, T]]))

        blk.sync(f_sy)
        blk.gpsimd(f_g)

    with nc.Block("ln1") as blk:
        ld_b2 = LD.n
        te_b = TE.n
        se_b = SE.n

        def f_s(s):
            s.wait_ge(LD.h, ld_b2)
            for cci in range(NCC):
                for mbi in range(4):
                    j = cci * 4 + mbi
                    sl = slice(mbi * 512, mbi * 512 + 512)
                    if j >= 2:
                        s.wait_ge(TE.h, te_b + 2 * (j - 1))
                    s.activation(xsq[:, j % 2, :], xloc[:, cci, sl],
                                 AF.Square).then_inc(SE.h, 1)
                    SE.n += 1

        def f_t(t):
            t.wait_ge(LD.h, ld_b2)
            for cci in range(NCC):
                for mbi in range(4):
                    j = cci * 4 + mbi
                    sl = slice(mbi * 512, mbi * 512 + 512)
                    t.wait_ge(SE.h, se_b + j + 1)
                    t.matmul(psA[:, sl], ones_col[:, :], xloc[:, cci, sl],
                             start=(cci == 0), stop=(cci == NCC - 1)
                             ).then_inc(TE.h, 1)
                    TE.n += 1
                    t.matmul(psB[:, sl], ones_col[:, :], xsq[:, j % 2, :],
                             start=(cci == 0), stop=(cci == NCC - 1)
                             ).then_inc(TE.h, 1)
                    TE.n += 1

        blk.scalar(f_s)
        blk.tensor(f_t)

    def srw(k):
        return strow[0:1, k * T:k * T + T]

    with nc.Block("ln2") as blk:
        def f_v(v):
            v.tensor_scalar(srw(MU), psA[:, :], INVC, None, AL.mult)
            v.tensor_scalar(srw(V2), psB[:, :], INVC, None, AL.mult)
            v.drain()
            v.tensor_tensor(srw(MR), srw(MU), srw(MU), AL.mult)
            v.drain()
            v.tensor_tensor(srw(V2), srw(V2), srw(MR), AL.subtract
                            ).then_inc(VE.h, 1)
            VE.n += 1

        def f_s(s):
            s.wait_ge(VE.h, VE.n)
            s.activation(srw(MR), srw(V2), AF.Sqrt, bias=LN_EPS).then_inc(SE.h, 1)
            SE.n += 1

        blk.vector(f_v)
        blk.scalar(f_s)

    with nc.Block("ln3") as blk:
        def f_v(v):
            v.wait_ge(SE.h, SE.n)
            v.reciprocal(srw(V2), srw(MR))
            v.drain()
            v.tensor_tensor(srw(MR), srw(MU), srw(V2), AL.mult).then_inc(VE.h, 1)
            VE.n += 1

        blk.vector(f_v)

    lnp.close()
    bbps = ps("bbps", [128, 512], cx=lnx)
    with nc.Block("ln4") as blk:
        te_b = TE.n
        ve_b = VE.n

        def f_t(t):
            t.wait_ge(VE.h, ve_b)
            for mbi in range(4):
                if mbi >= 1:
                    t.wait_ge(VE.h, ve_b + 2 * mbi)
                t.matmul(bbps[:, :], ones_row[:, :],
                         strow[0:1, V2 * T + mbi * 512:V2 * T + mbi * 512 + 512],
                         start=True, stop=True).then_inc(TE.h, 1)
                TE.n += 1
                t.wait_ge(VE.h, ve_b + 2 * mbi + 1)
                t.matmul(bbps[:, :], ones_row[:, :],
                         strow[0:1, MR * T + mbi * 512:MR * T + mbi * 512 + 512],
                         start=True, stop=True).then_inc(TE.h, 1)
                TE.n += 1

        def f_v(v):
            for mbi in range(4):
                sl = slice(mbi * 512, mbi * 512 + 512)
                v.wait_ge(TE.h, te_b + 2 * mbi + 1)
                v.tensor_copy(rbb[:, sl], bbps[:, :]).then_inc(VE.h, 1)
                VE.n += 1
                v.wait_ge(TE.h, te_b + 2 * mbi + 2)
                v.tensor_copy(mbb[:, sl], bbps[:, :]).then_inc(VE.h, 1)
                VE.n += 1

        blk.tensor(f_t)
        blk.vector(f_v)

    with nc.Block("ln5") as blk:
        def f_v(v):
            for cci in range(NCC):
                v.tensor_tensor(xscr[:, :], xloc[:, cci, :], rbb[:, :], AL.mult)
                v.drain()
                v.tensor_tensor(xscr[:, :], xscr[:, :], mbb[:, :], AL.subtract)
                v.drain()
                v.tensor_copy(xnh[:, cci, :], xscr[:, :])
                v.drain()
                v.tensor_tensor(xnl[:, cci, :], xscr[:, :], xnh[:, cci, :],
                                AL.subtract)
                v.drain()

        blk.vector(f_v)

    lnx.close()
    if _PHASE_LIMIT <= 2:
        ctx.close()
        return nc


    rx = ExitStack()
    xtile = [sb(f"xtile{i}", [128, 2048], cx=rx) for i in range(2)]
    sqrow = sb("sqrow", [1, 2, 2048], cx=rx)
    xbf = sb("xbf", [128, 2048], BF16, cx=rx)
    xsqb = sb("xsqb", [128, 2048], BF16, cx=rx)
    lg = sb("lg", [10, TG], BF16, cx=rx)
    tmr = sb("tmr", [128, NCHG, 10], cx=rx)
    va = sb("va", [128, NCHG], cx=rx)
    vb = sb("vb", [128, NCHG], cx=rx)
    vv = sb("vv", [128, NCHG, 8], cx=rx)
    w1 = sb("w1", [128, NCHG, 8], cx=rx)
    w2 = sb("w2", [128, NCHG, 8], cx=rx)
    w3 = sb("w3", [128, NCHG, 8], cx=rx)
    r1c = sb("r1c", [128, NCHG], cx=rx)
    r2c = sb("r2c", [128, NCHG], cx=rx)
    csrow = sb("csrow", [1, 8], cx=rx)
    wmrow = sb("wmrow", [1, 24], cx=rx)
    rp0 = ExitStack()
    lgps = ps("lgps", [9, 2048], cx=rp0)
    sqps = ps("sqps", [1, 2048], cx=rp0)

    NTB = TG // 2048
    with nc.Block("r0") as blk:
        ld_b = LD.n
        se_b = SE.n
        te_b = TE.n
        ve_b = VE.n

        xts_b = XTS.i

        def f_sy(sy):
            for tb in range(NTB):
                for cci in range(NCC):
                    i = tb * NCC + cci
                    if i >= 2:
                        sy.wait_ge(SE.h, se_b + 2 * (i - 1))
                    XTS.issue(sy, xtile[i % 2][:, :],
                              bass.AP(xT.tensor, cci * 128 * TG + tb * 2048,
                                      [[TG, 128], [1, 2048]]))

        def f_s(s):
            for tb in range(NTB):
                for cci in range(NCC):
                    i = tb * NCC + cci
                    XTS.wait(s, xts_b + i)
                    if i >= 1:
                        s.wait_ge(TE.h, te_b + 8 * i)
                    s.copy(xbf[:, :], xtile[i % 2][:, :]).then_inc(SE.h, 1)
                    SE.n += 1
                    s.drain()
                    s.activation(xsqb[:, :], xbf[:, :], AF.Square).then_inc(SE.h, 1)
                    SE.n += 1

        def f_t(t):
            for tb in range(NTB):
                if tb >= 1:
                    t.wait_ge(VE.h, ve_b + 2 * tb)
                for cci in range(NCC):
                    i = tb * NCC + cci
                    t.wait_ge(SE.h, se_b + 2 * (i + 1))
                    for mbi in range(4):
                        sl = slice(mbi * 512, mbi * 512 + 512)
                        t.matmul(lgps[:, sl], gstat[:, cci, :], xbf[:, sl],
                                 start=(cci == 0), stop=(cci == NCC - 1)
                                 ).then_inc(TE.h, 1)
                        TE.n += 1
                        t.matmul(sqps[:, sl], ones_colb[:, :], xsqb[:, sl],
                                 start=(cci == 0), stop=(cci == NCC - 1)
                                 ).then_inc(TE.h, 1)
                        TE.n += 1

        sqs_b = SQS.i

        def f_v(v):
            for tb in range(NTB):
                v.wait_ge(TE.h, te_b + 64 * (tb + 1))
                if tb >= 2:
                    SQS.wait(v, sqs_b + tb - 2)
                v.tensor_copy(lg[0:9, tb * 2048:tb * 2048 + 2048], lgps[:, :])
                v.tensor_copy(sqrow[0:1, tb % 2, :], sqps[:, :]
                              ).then_inc(VE.h, 2)
                VE.n += 2

        def f_g(g):
            for tb in range(NTB):
                g.wait_ge(VE.h, ve_b + 2 * (tb + 1))
                SQS.issue(g, lg[9:10, tb * 2048:tb * 2048 + 2048],
                          sqrow[0:1, tb % 2, :])

        blk.sync(f_sy)
        blk.scalar(f_s)
        blk.tensor(f_t)
        blk.vector(f_v)
        blk.gpsimd(f_g)

    if _PHASE_LIMIT <= 21:
        rx.close()
        ctx.close()
        return nc

    rp0.close()
    trps2 = [ps(f"trps{i}", [128, 1024], cx=rx, dt=BF16) for i in range(2)]
    with nc.Block("r1") as blk:
        te_b = TE.n
        ve_b = VE.n

        def f_t(t):
            SQS.wait_all(t)
            for grp in range(16):
                if grp >= 2:
                    t.wait_ge(VE.h, ve_b + grp - 1)
                for j in range(8):
                    ch = grp * 8 + j
                    t.matmul(trps2[grp % 2][0:128, j * 10:j * 10 + 10],
                             lg[:, ch * 128:ch * 128 + 128],
                             idb16[0:10, 0:10], is_transpose=True
                             ).then_inc(TE.h, 1)
                    TE.n += 1

        def f_v(v):
            for grp in range(16):
                v.wait_ge(TE.h, te_b + 8 * (grp + 1))
                v.tensor_copy(tmr[:, grp * 8:grp * 8 + 8, :].rearrange(
                    "p a b -> p (a b)"), trps2[grp % 2][:, 0:80]).then_inc(VE.h, 1)
                VE.n += 1

        blk.tensor(f_t)
        blk.vector(f_v)

    if _PHASE_LIMIT <= 22:
        rx.close()
        ctx.close()
        return nc

    s8ps = ps("s8ps", [128, 512], cx=rx)
    with nc.Block("r2") as blk:
        def f_t(t):
            for cci in range(NCC):
                t.matmul(s8ps[0:1, 0:8], ones_col[:, :], gw32[:, cci, :],
                         start=(cci == 0), stop=(cci == NCC - 1)).then_inc(TE.h, 1)
                TE.n += 1

        def f_v(v):
            v.wait_ge(TE.h, TE.n)
            v.tensor_copy(csrow[0:1, :], s8ps[0:1, 0:8]).then_inc(VE.h, 1)
            VE.n += 1

        blk.tensor(f_t)
        blk.vector(f_v)

    with nc.Block("r3") as blk:
        def f_t(t):
            t.wait_ge(VE.h, VE.n)
            t.matmul(s8ps[:, 0:8], ones_row[:, :], csrow[0:1, :],
                     start=True, stop=True)
            t.matmul(s8ps[:, 8:16], ones_row[:, :], b8[0:1, :],
                     start=True, stop=True).then_inc(TE.h, 1)
            TE.n += 1

        def f_v(v):
            v.wait_ge(TE.h, TE.n)
            v.tensor_copy(cs8[:, :], s8ps[:, 0:8])
            v.tensor_copy(b8[:, :], s8ps[:, 8:16])

        blk.tensor(f_t)
        blk.vector(f_v)

    def tmsl(k):
        return bass.AP(tmr.tensor, k, [[NCHG * 10, 128], [10, NCHG], [1, 1]])

    def bcol(ap_t):
        return bass.AP(ap_t.tensor, ap_t.offset, [[NCHG, 128], [1, NCHG], [0, 8]])

    e8b = bass.AP(e8t.tensor, 0, [[8, 128], [0, NCHG], [1, 8]])
    r8b = bass.AP(r8t.tensor, 0, [[8, 128], [0, NCHG], [1, 8]])
    b8b = bass.AP(b8.tensor, 0, [[8, 128], [0, NCHG], [1, 8]])
    cs8b = bass.AP(cs8.tensor, 0, [[8, 128], [0, NCHG], [1, 8]])

    with nc.Block("r4") as blk:
        def f_v(v):
            v.tensor_scalar(va[:, :].rearrange("p (a b) -> p a b", b=1),
                            tmsl(8), INVC, None, AL.mult)
            v.drain()
            v.tensor_tensor(vb[:, :], va[:, :], va[:, :], AL.mult)
            v.tensor_scalar(r1c[:, :].rearrange("p (a b) -> p a b", b=1),
                            tmsl(9), INVC, None, AL.mult)
            v.drain()
            v.tensor_tensor(r1c[:, :], r1c[:, :], vb[:, :], AL.subtract
                            ).then_inc(VE.h, 1)
            VE.n += 1

        def f_s(s):
            s.wait_ge(VE.h, VE.n)
            s.activation(r1c[:, :], r1c[:, :], AF.Sqrt, bias=LN_EPS
                         ).then_inc(SE.h, 1)
            SE.n += 1

        blk.vector(f_v)
        blk.scalar(f_s)

    with nc.Block("r5") as blk:
        def f_v(v):
            v.wait_ge(SE.h, SE.n)
            v.reciprocal(r2c[:, :], r1c[:, :])
            v.drain()
            v.tensor_tensor(vb[:, :], va[:, :], r2c[:, :], AL.mult)
            v.drain()
            v.tensor_copy(va[:, :], r2c[:, :])
            v.drain()
            v.tensor_tensor(w1[:, :, :], tmr[:, :, 0:8], bcol(va), AL.mult)
            v.drain()
            v.tensor_tensor(w2[:, :, :], bcol(vb), cs8b, AL.mult)
            v.drain()
            v.tensor_tensor(w1[:, :, :], w1[:, :, :], w2[:, :, :], AL.subtract)
            v.drain()
            v.tensor_tensor(w1[:, :, :], w1[:, :, :], b8b, AL.add).then_inc(VE.h, 1)
            VE.n += 1

        def f_s(s):
            s.wait_ge(VE.h, VE.n)
            s.activation(vv[:, :, :], w1[:, :, :], AF.Sigmoid).then_inc(SE.h, 1)
            SE.n += 1

        blk.vector(f_v)
        blk.scalar(f_s)

    with nc.Block("r6") as blk:
        def f_v(v):
            v.wait_ge(SE.h, SE.n)
            v.reduce_max(r1c[:, :].rearrange("p (a b) -> p a b", b=1),
                         vv[:, :, :], axis=AX)
            v.drain()
            v.tensor_tensor(w1[:, :, :], vv[:, :, :], bcol(r1c), AL.subtract
                            ).then_inc(VE.h, 1)
            VE.n += 1

        def f_s(s):
            s.wait_ge(VE.h, VE.n)
            s.activation(w1[:, :, :], w1[:, :, :], AF.Relu, scale=HUGE,
                         bias=1.0).then_inc(SE.h, 1)
            SE.n += 1

        blk.vector(f_v)
        blk.scalar(f_s)

    with nc.Block("r7") as blk:
        def f_v(v):
            v.wait_ge(SE.h, SE.n)
            v.tensor_tensor(w2[:, :, :], w1[:, :, :], r8b, AL.mult)
            v.drain()
            v.tensor_scalar(w2[:, :, :], w2[:, :, :], -1.0, 8.0, AL.mult, AL.add)
            v.drain()
            v.tensor_reduce(r2c[:, :].rearrange("p (a b) -> p a b", b=1),
                            w2[:, :, :], AX, AL.min)
            v.drain()
            v.tensor_tensor(w2[:, :, :], e8b, bcol(r2c), AL.subtract
                            ).then_inc(VE.h, 1)
            VE.n += 1

        def f_s(s):
            s.wait_ge(VE.h, VE.n)
            s.activation(w2[:, :, :], w2[:, :, :], AF.Abs)
            s.drain()
            s.activation(w2[:, :, :], w2[:, :, :], AF.Relu, scale=-HUGE,
                         bias=1.0).then_inc(SE.h, 1)
            SE.n += 1

        blk.vector(f_v)
        blk.scalar(f_s)

    with nc.Block("r8") as blk:
        def f_v(v):
            v.wait_ge(SE.h, SE.n)
            v.scalar_tensor_tensor(w3[:, :, :], w2[:, :, :], -HUGE, vv[:, :, :],
                                   AL.mult, AL.add)
            v.drain()
            v.reduce_max(r2c[:, :].rearrange("p (a b) -> p a b", b=1),
                         w3[:, :, :], axis=AX)
            v.drain()
            v.tensor_tensor(w3[:, :, :], w3[:, :, :], bcol(r2c), AL.subtract
                            ).then_inc(VE.h, 1)
            VE.n += 1

        def f_s(s):
            s.wait_ge(VE.h, VE.n)
            s.activation(w3[:, :, :], w3[:, :, :], AF.Relu, scale=HUGE,
                         bias=1.0).then_inc(SE.h, 1)
            SE.n += 1

        blk.vector(f_v)
        blk.scalar(f_s)

    with nc.Block("r9") as blk:
        def f_v(v):
            v.wait_ge(SE.h, SE.n)
            v.tensor_tensor(r1c[:, :], r1c[:, :], r2c[:, :], AL.add)
            v.tensor_tensor(w3[:, :, :], w3[:, :, :], r8b, AL.mult)
            v.drain()
            v.tensor_scalar(w3[:, :, :], w3[:, :, :], -1.0, 8.0, AL.mult, AL.add)
            v.tensor_scalar(r1c[:, :], r1c[:, :], 1.0, 1e-8, AL.mult, AL.add)
            v.drain()
            v.tensor_reduce(r2c[:, :].rearrange("p (a b) -> p a b", b=1),
                            w3[:, :, :], AX, AL.min)
            v.reciprocal(r1c[:, :], r1c[:, :])
            v.drain()
            v.tensor_tensor(w3[:, :, :], e8b, bcol(r2c), AL.subtract
                            ).then_inc(VE.h, 1)
            VE.n += 1

        def f_s(s):
            s.wait_ge(VE.h, VE.n)
            s.activation(w3[:, :, :], w3[:, :, :], AF.Abs)
            s.drain()
            s.activation(w3[:, :, :], w3[:, :, :], AF.Relu, scale=-HUGE,
                         bias=1.0).then_inc(SE.h, 1)
            SE.n += 1

        blk.vector(f_v)
        blk.scalar(f_s)

    if _PHASE_LIMIT <= 23:
        rx.close()
        ctx.close()
        return nc

    smps = ps("smps", [1, 512], cx=rx)
    with nc.Block("r10") as blk:
        def f_v(v):
            v.wait_ge(SE.h, SE.n)
            v.tensor_tensor(w2[:, :, :], w2[:, :, :], w3[:, :, :], AL.add)
            v.drain()
            v.tensor_tensor(w1[:, :, :], w2[:, :, :], vv[:, :, :], AL.mult)
            v.drain()
            v.tensor_tensor(w1[:, :, :], w1[:, :, :], bcol(r1c), AL.mult)
            v.drain()
            v.tensor_reduce(w3[:, 0, :].rearrange("p (a b) -> p a b", b=1),
                            bass.AP(w1.tensor, 0,
                                    [[NCHG * 8, 128], [1, 8], [8, NCHG]]),
                            AX, AL.add)
            v.tensor_reduce(w3[:, 1, :].rearrange("p (a b) -> p a b", b=1),
                            bass.AP(w2.tensor, 0,
                                    [[NCHG * 8, 128], [1, 8], [8, NCHG]]),
                            AX, AL.add).then_inc(VE.h, 1)
            VE.n += 1

        def f_t(t):
            t.wait_ge(VE.h, VE.n)
            t.matmul(smps[0:1, 0:8], ones_col[:, :], w3[:, 0, :],
                     start=True, stop=True)
            t.matmul(smps[0:1, 8:16], ones_col[:, :], w3[:, 1, :],
                     start=True, stop=True).then_inc(TE.h, 1)
            TE.n += 1

        blk.vector(f_v)
        blk.tensor(f_t)

    with nc.Block("r11") as blk:
        def f_v(v):
            v.wait_ge(TE.h, TE.n)
            v.tensor_copy(wmrow[0:1, 0:16], smps[0:1, 0:16])
            v.drain()
            v.tensor_scalar(wmrow[0:1, 8:16], wmrow[0:1, 8:16], 1.0, None,
                            AL.max)
            v.drain()
            v.reciprocal(wmrow[0:1, 8:16], wmrow[0:1, 8:16])
            v.drain()
            v.tensor_tensor(wmrow[0:1, 16:24], wmrow[0:1, 0:8],
                            wmrow[0:1, 8:16], AL.mult).then_inc(VE.h, 1)
            VE.n += 1

        def f_t(t):
            t.wait_ge(VE.h, VE.n)
            t.matmul(s8ps[:, 0:8], ones_row[:, :], wmrow[0:1, 16:24],
                     start=True, stop=True).then_inc(TE.h, 1)
            TE.n += 1

        blk.vector(f_v)
        blk.tensor(f_t)

    with nc.Block("r12") as blk:
        def f_v(v):
            v.wait_ge(TE.h, TE.n)
            v.tensor_copy(wm8[:, :], s8ps[:, 0:8])
            v.drain()
            v.tensor_tensor(sc[:, :, :], w2[:, 0:NCHL, :],
                            bass.AP(wm8.tensor, 0, [[8, 128], [0, NCHL], [1, 8]]),
                            AL.mult)

        blk.vector(f_v)

    rx.close()
    if _PHASE_LIMIT <= 3:
        ctx.close()
        return nc


    fx = ExitStack()
    gsl = [sb("gsl0", [128, NCC, 128], cx=fx)] * 2
    usl = [sb("usl0", [128, NCC, 128], cx=fx)] * 2
    dsl = [sb("dsl0", [128, 4, C], cx=fx)] * 2
    gslb = [sb(f"gslb{i}", [128, 2, NCC, 128], BF16, cx=fx) for i in range(2)]
    uslb = [sb(f"uslb{i}", [128, 2, NCC, 128], BF16, cx=fx) for i in range(2)]
    dslb = [sb("dslb0", [128, 2, 4, C], BF16, cx=fx)] * 2
    auh = sb("auh", [128, 4, T], BF16, cx=fx)
    aul = sb("aul", [128, 4, T], BF16, cx=fx)
    a32 = sb("a32", [128, 512], cx=fx)
    sgb = [sb(f"sgb{i}", [128, 512], cx=fx) for i in range(2)]
    a1b = [sb("a1b0", [128, 512], cx=fx)] * 2
    scb = sb("scb", [128, T], cx=fx)
    sctp = sb("sctp", [16, 128], cx=fx)
    scrow = sb("scrow", [1, T], cx=fx)
    ybuf = sb("ybuf", [128, 4, 512], cx=fx)
    gps = [ps(f"gps{i}", [128, 512], cx=fx) for i in range(2)]
    ups = [ps(f"ups{i}", [128, 512], cx=fx) for i in range(2)]
    yps = [ps(f"yps{i}", [128, 512], cx=fx) for i in range(2)]
    scps = ps("scps", [128, 512], cx=fx)
    sctrp = ps("sctrp", [128, 512], cx=fx)

    NTT = 4

    def unit_wsrc(u):
        if u < E:
            return (egwT[u], euwT[u], edwT[u], IEXP)
        hh = u - E
        return (sgwT[:, hh * 512:hh * 512 + 512],
                suwT[:, hh * 512:hh * 512 + 512],
                sdwT[hh * 512:hh * 512 + 512, :], 2 * IEXP)

    prev_ve_at_split = None
    for u in range(10):
        if u >= int(os.environ.get("KUNITS", "10")):
            break
        gw_src, uw_src, dw_src, rs = unit_wsrc(u)
        has_sc = u < E

        if has_sc:
            with nc.Block(f"fsc{u}") as blk:
                def f_t(t, u=u):
                    t.matmul(sctrp[0:16, 0:128],
                             bass.AP(sc.tensor, u, [[NCHL * 8, 128], [8, NCHL]]),
                             id128[:, :], is_transpose=True).then_inc(TE.h, 1)
                    TE.n += 1

                def f_v(v):
                    v.wait_ge(TE.h, TE.n)
                    v.tensor_copy(sctp[:, :], sctrp[0:16, 0:128]).then_inc(VE.h, 1)
                    VE.n += 1

                blk.tensor(f_t)
                blk.vector(f_v)

            with nc.Block(f"fsr{u}") as blk:
                def f_sy(sy):
                    sy.wait_ge(VE.h, VE.n)
                    sy.dma_start(
                        bass.AP(scrow.tensor, 0, [[T, 1], [128, 16], [1, 128]]),
                        sctp[:, :]).then_inc(LD.h, 16)
                    LD.n += 16

                blk.sync(f_sy)

            with nc.Block(f"fsb{u}") as blk:
                te_b = TE.n
                ve_b = VE.n

                def f_t(t):
                    t.wait_ge(LD.h, LD.n)
                    for mbi in range(4):
                        if mbi >= 1:
                            t.wait_ge(VE.h, ve_b + mbi)
                        t.matmul(scps[:, :], ones_row[:, :],
                                 scrow[0:1, mbi * 512:mbi * 512 + 512],
                                 start=True, stop=True).then_inc(TE.h, 1)
                        TE.n += 1

                def f_v(v):
                    for mbi in range(4):
                        v.wait_ge(TE.h, te_b + mbi + 1)
                        v.tensor_copy(scb[:, mbi * 512:mbi * 512 + 512],
                                      scps[:, :]).then_inc(VE.h, 1)
                        VE.n += 1

                blk.tensor(f_t)
                blk.vector(f_v)

        with nc.Block(f"fgu{u}") as blk:
            ld_b = LD.n
            te_b = TE.n
            se_b = SE.n
            ve_b = VE.n

            wfs_b = WFS.i
            dds_tok_holder = []
            ve_at_split = {}
            ve_at_a = {}
            vc = ve_b + 1
            vc += 1
            ve_at_split[0] = vc
            for ic in range(4):
                for tt in range(NTT):
                    vc += 2
                    ve_at_a[ic * NTT + tt] = vc
                    if tt == 0 and ic < 3:
                        vc += 1
                        ve_at_split[ic + 1] = vc
            ve_end = vc

            def f_sy(sy, gw_src=gw_src, uw_src=uw_src, dw_src=dw_src, u=u,
                     rs=rs, pvs=prev_ve_at_split):
                dds_tok_holder.append(
                    DDS.issue(sy, dsl[u % 2][:, :, :],
                              bass.AP(dw_src.tensor, dw_src.offset,
                                      [[C, 128], [128 * C, 4], [1, C]])))
                for ic in range(4):
                    if ic >= 1:
                        sy.wait_ge(VE.h, ve_at_split[ic - 1])
                    elif pvs is not None:
                        sy.wait_ge(VE.h, pvs[3])
                    WFS.issue(sy, gsl[ic % 2][:, :, :],
                              bass.AP(gw_src.tensor, gw_src.offset + ic * 128,
                                      [[rs, 128], [128 * rs, NCC], [1, 128]]))
                    WFS.issue(sy, usl[ic % 2][:, :, :],
                              bass.AP(uw_src.tensor, uw_src.offset + ic * 128,
                                      [[rs, 128], [128 * rs, NCC], [1, 128]]))

            def f_v(v, has_sc=has_sc, te_b=te_b, se_b=se_b, u=u,
                    wfs_b=wfs_b):
                DDS.wait(v, dds_tok_holder[0])
                v.tensor_copy(dslb[u % 2][:, 0, :, :], dsl[u % 2][:, :, :])
                v.drain()
                v.tensor_tensor(dslb[u % 2][:, 1, :, :], dsl[u % 2][:, :, :],
                                dslb[u % 2][:, 0, :, :], AL.subtract
                                ).then_inc(VE.h, 1)
                VE.n += 1

                def split(ic):
                    WFS.wait(v, wfs_b + 2 * ic)
                    WFS.wait(v, wfs_b + 2 * ic + 1)
                    if ic >= 2:
                        v.wait_ge(TE.h, te_b + 192 * (ic - 1))
                    v.tensor_copy(gslb[ic % 2][:, 0, :, :], gsl[ic % 2][:, :, :])
                    v.tensor_copy(uslb[ic % 2][:, 0, :, :], usl[ic % 2][:, :, :])
                    v.drain()
                    v.tensor_tensor(gslb[ic % 2][:, 1, :, :],
                                    gsl[ic % 2][:, :, :],
                                    gslb[ic % 2][:, 0, :, :], AL.subtract)
                    v.tensor_tensor(uslb[ic % 2][:, 1, :, :],
                                    usl[ic % 2][:, :, :],
                                    uslb[ic % 2][:, 0, :, :], AL.subtract
                                    ).then_inc(VE.h, 1)
                    VE.n += 1
                    assert VE.n == ve_at_split[ic]

                split(0)
                for ic in range(4):
                    for tt in range(NTT):
                        it = ic * NTT + tt
                        sl = slice(tt * 512, tt * 512 + 512)
                        v.wait_ge(TE.h, te_b + 48 * (it + 1))
                        v.wait_ge(SE.h, se_b + it + 1)
                        v.drain()
                        if has_sc:
                            v.tensor_tensor(a1b[it % 2][:, :], ups[it % 2][:, :],
                                            scb[:, sl], AL.mult)
                            v.drain()
                            v.tensor_tensor(a32[:, :], a1b[it % 2][:, :],
                                            sgb[it % 2][:, :], AL.mult)
                        else:
                            v.tensor_tensor(a32[:, :], ups[it % 2][:, :],
                                            sgb[it % 2][:, :], AL.mult)
                        v.drain()
                        v.tensor_copy(auh[:, ic, sl], a32[:, :])
                        v.drain()
                        v.tensor_tensor(aul[:, ic, sl], a32[:, :],
                                        auh[:, ic, sl], AL.subtract
                                        ).then_inc(VE.h, 2)
                        VE.n += 2
                        assert VE.n == ve_at_a[it]
                        if tt == 0 and ic < 3:
                            split(ic + 1)

            def f_t(t, te_b=te_b, ve_b=ve_b):
                for ic in range(4):
                    t.wait_ge(VE.h, ve_at_split[ic])
                    for tt in range(NTT):
                        it = ic * NTT + tt
                        sl = slice(tt * 512, tt * 512 + 512)
                        if it >= 2:
                            t.wait_ge(VE.h, ve_at_a[it - 2])
                        for cci in range(NCC):
                            t.matmul(gps[it % 2][:, :],
                                     gslb[ic % 2][:, 0, cci, :],
                                     xnh[:, cci, sl], start=(cci == 0),
                                     stop=False).then_inc(TE.h, 1)
                            TE.n += 1
                            t.matmul(gps[it % 2][:, :],
                                     gslb[ic % 2][:, 0, cci, :],
                                     xnl[:, cci, sl], start=False,
                                     stop=False).then_inc(TE.h, 1)
                            TE.n += 1
                            t.matmul(gps[it % 2][:, :],
                                     gslb[ic % 2][:, 1, cci, :],
                                     xnh[:, cci, sl], start=False,
                                     stop=(cci == NCC - 1)).then_inc(TE.h, 1)
                            TE.n += 1
                        for cci in range(NCC):
                            t.matmul(ups[it % 2][:, :],
                                     uslb[ic % 2][:, 0, cci, :],
                                     xnh[:, cci, sl], start=(cci == 0),
                                     stop=False).then_inc(TE.h, 1)
                            TE.n += 1
                            t.matmul(ups[it % 2][:, :],
                                     uslb[ic % 2][:, 0, cci, :],
                                     xnl[:, cci, sl], start=False,
                                     stop=False).then_inc(TE.h, 1)
                            TE.n += 1
                            t.matmul(ups[it % 2][:, :],
                                     uslb[ic % 2][:, 1, cci, :],
                                     xnh[:, cci, sl], start=False,
                                     stop=(cci == NCC - 1)).then_inc(TE.h, 1)
                            TE.n += 1

            def f_s(s, te_b=te_b, ve_b=ve_b):
                for it in range(16):
                    s.wait_ge(TE.h, te_b + 48 * it + 24)
                    if it >= 2:
                        s.wait_ge(VE.h, ve_at_a[it - 2])
                    s.activation(sgb[it % 2][:, :], gps[it % 2][:, :],
                                 AF.Silu).then_inc(SE.h, 1)
                    SE.n += 1

            blk.sync(f_sy)
            blk.vector(f_v)
            blk.tensor(f_t)
            blk.scalar(f_s)

        prev_ve_at_split = ve_at_split

        with nc.Block(f"fdn{u}") as blk:
            te_b = TE.n
            se_b = SE.n
            ads_b = ADS.i

            def f_t(t, u=u, te_b=te_b, se_b=se_b):
                for tt in range(NTT):
                    for co in range(NCC):
                        j = tt * NCC + co
                        sl = slice(tt * 512, tt * 512 + 512)
                        if j >= 2:
                            t.wait_ge(SE.h, se_b + j - 1)
                        for ic in range(4):
                            dh = dslb[u % 2][:, 0, ic, co * 128:co * 128 + 128]
                            dl = dslb[u % 2][:, 1, ic, co * 128:co * 128 + 128]
                            t.matmul(yps[j % 2][:, :], dh, auh[:, ic, sl],
                                     start=(ic == 0), stop=False
                                     ).then_inc(TE.h, 1)
                            TE.n += 1
                            t.matmul(yps[j % 2][:, :], dh, aul[:, ic, sl],
                                     start=False, stop=False).then_inc(TE.h, 1)
                            TE.n += 1
                            t.matmul(yps[j % 2][:, :], dl, auh[:, ic, sl],
                                     start=False, stop=(ic == 3)
                                     ).then_inc(TE.h, 1)
                            TE.n += 1

            def f_s(s, te_b=te_b, ads_b=ads_b):
                for j in range(32):
                    s.wait_ge(TE.h, te_b + 12 * (j + 1))
                    if ads_b + j >= 4:
                        ADS.wait(s, ads_b + j - 4)
                    s.copy(ybuf[:, (ads_b + j) % 4, :], yps[j % 2][:, :]
                           ).then_inc(SE.h, 1)
                    SE.n += 1

            def f_g(g, se_b=se_b, ads_b=ads_b):
                ADS.wait_all(g)
                for tt in range(NTT):
                    for co in range(NCC):
                        j = tt * NCC + co
                        g.wait_ge(SE.h, se_b + j + 1)
                        if os.environ.get("KSUB") == "3":
                            tok = ADS.issue(g, out[co * 128:co * 128 + 128,
                                                   tt * 512:tt * 512 + 512],
                                            ybuf[:, (ads_b + j) % 4, :])
                        else:
                            tok = ADS.issue(g, out[co * 128:co * 128 + 128,
                                                   tt * 512:tt * 512 + 512],
                                            ybuf[:, (ads_b + j) % 4, :],
                                            accum_op=AL.add)
                        assert tok == ads_b + j

            blk.tensor(f_t)
            blk.scalar(f_s)
            blk.gpsimd(f_g)

    if _PHASE_LIMIT == 31:
        with nc.Block("dbg31") as blk:
            def f_sy(sy):
                sy.wait_ge(VE.h, VE.n)
                sy.wait_ge(SE.h, SE.n)
                sy.wait_ge(TE.h, TE.n)
                kdt = os.environ.get("KDT", "auh")
                if kdt == "xnh":
                    sy.dma_start(dbg[:, :], xnh[:, 0, :]).then_inc(LD.h, 16)
                elif kdt == "aul":
                    sy.dma_start(dbg[:, :], aul[:, 0, :]).then_inc(LD.h, 16)
                elif kdt == "scb":
                    sy.dma_start(dbg[:, 0:1024].rearrange("p (a b) -> p a b", a=1),
                                 bass.AP(scb.tensor, 0, [[2048, 128], [2, 1024]])
                                 ).then_inc(LD.h, 16)
                elif kdt == "gslb":
                    sy.dma_start(dbg[:, 0:1024],
                                 gslb[0][:, 0, :, :].rearrange("p a b -> p (a b)")
                                 ).then_inc(LD.h, 16)
                elif kdt == "ybuf":
                    sy.dma_start(dbg[:, 0:2048],
                                 ybuf[:, :, :].rearrange("p a b -> p (a b)")
                                 ).then_inc(LD.h, 16)
                else:
                    sy.dma_start(dbg[:, :], auh[:, 0, :]).then_inc(LD.h, 16)
                LD.n += 16
                sy.wait_ge(LD.h, LD.n)

            blk.sync(f_sy)
    fx.close()
    if _PHASE_LIMIT in (31, 4) and os.environ.get("KSUB") != "4":
        with nc.Block("fin31") as blk:
            def f_g(g):
                ADS.wait_all(g)

            blk.gpsimd(f_g)
    if _PHASE_LIMIT <= 4 or _PHASE_LIMIT == 31:
        ctx.close()
        return nc


    ax = ExitStack()
    wsl = [sb("wsl0", [128, NCC, 256], cx=ax)] * 2
    wslb = [sb(f"wslb{i}", [128, 2, NCC, 256], BF16, cx=ax) for i in range(2)]
    qtm = sb("qtm", [128, 16, 256], cx=ax)
    kpkh = sb("kpkh", [128, 2, T], BF16, cx=ax)
    kpkl = sb("kpkl", [128, 2, T], BF16, cx=ax)
    qEh = sb("qEh", [128, 2, T], BF16, cx=ax)
    qEl = sb("qEl", [128, 2, T], BF16, cx=ax)
    qOh = sb("qOh", [128, 2, T], BF16, cx=ax)
    qOl = sb("qOl", [128, 2, T], BF16, cx=ax)
    vth = sb("vth", [128, 2, 16, 132], BF16, cx=ax)
    vtl = sb("vtl", [128, 2, 16, 132], BF16, cx=ax)
    rsc1 = sb("rsc1", [128, 4, 256], cx=ax)
    rsc2 = sb("rsc2", [128, 4, 256], cx=ax)
    rsc3 = sb("rsc3", [128, 4, 256], cx=ax)
    msq = sb("msq", [128, 16, 4], cx=ax)
    p32 = [sb(f"p32{i}", [128, 256], cx=ax) for i in range(4)]
    pbh = [sb(f"pbh{i}", [128, 256], BF16, cx=ax) for i in range(4)]
    pbl = [sb(f"pbl{i}", [128, 256], BF16, cx=ax) for i in range(4)]
    oc1 = sb("oc1", [128, 2, 128], cx=ax)
    oc2 = sb("oc2", [128, 2, 128], cx=ax)
    occ = sb("occ", [128, 16], cx=ax)
    olnb = sb("olnb", [128, 256], cx=ax)
    oth = sb("oth", [128, 2, 256], BF16, cx=ax)
    otl = sb("otl", [128, 2, 256], BF16, cx=ax)
    sps = [ps(f"sps{i}", [128, 512], cx=ax) for i in range(3)]
    avps = [ps(f"avps{i}", [128, 512], cx=ax) for i in range(4)]
    otps = ps("otps", [128, 512], cx=ax)
    prps = [sps[0], sps[1], sps[2]]

    with nc.Block("aqz") as blk:
        def f_v(v):
            ADS.wait_all(v)
            v.memset(qEh[64:128, :, :], 0.0)
            v.memset(qEl[64:128, :, :], 0.0)
            v.memset(qOh[0:64, :, :], 0.0)
            v.memset(qOl[0:64, :, :], 0.0)

        def f_sy(sy):
            ADS.wait_all(sy)

        blk.vector(f_v)
        blk.sync(f_sy)

    for hg in range(4):
        for ti, (wt, name) in enumerate([(wqT, "q"), (wkT, "k"), (wvT, "v")]):
            with nc.Block(f"apl{hg}{name}") as blk:
                was_tok = []

                def f_sy(sy, wt=wt, ti=ti, hg=hg):
                    sy.wait_ge(VE.h, VE.n)
                    was_tok.append(
                        WAS.issue(sy, wsl[ti % 2][:, :, :],
                                  bass.AP(wt.tensor, hg * 256,
                                          [[C, 128], [128 * C, NCC], [1, 256]])))

                blk.sync(f_sy)

            with nc.Block(f"apj{hg}{name}") as blk:
                te_b = TE.n
                se_b = SE.n
                ve_b = VE.n

                def f_vsp(v, ti=ti):
                    WAS.wait(v, was_tok[0])
                    v.tensor_copy(wslb[ti % 2][:, 0, :, :], wsl[ti % 2][:, :, :])
                    v.drain()
                    v.tensor_tensor(wslb[ti % 2][:, 1, :, :],
                                    wsl[ti % 2][:, :, :],
                                    wslb[ti % 2][:, 0, :, :], AL.subtract
                                    ).then_inc(VE.h, 1)
                    VE.n += 1

                def f_t(t, ti=ti, se_b=se_b, ve_b=ve_b, name=name):
                    t.wait_ge(VE.h, ve_b + 1)
                    for ts in range(16):
                        if ts >= 3:
                            t.wait_ge(SE.h, se_b + ts - 2)
                            if name == "v":
                                t.wait_ge(VE.h, ve_b + 1 + ts - 2)
                        for cci in range(NCC):
                            xh_c = xnh[:, cci, ts * 128:ts * 128 + 128]
                            xl_c = xnl[:, cci, ts * 128:ts * 128 + 128]
                            wh_c = wslb[ti % 2][:, 0, cci, :]
                            wl_c = wslb[ti % 2][:, 1, cci, :]
                            t.matmul(prps[ts % 3][:, 0:256], xh_c, wh_c,
                                     start=(cci == 0), stop=False
                                     ).then_inc(TE.h, 1)
                            TE.n += 1
                            t.matmul(prps[ts % 3][:, 0:256], xh_c, wl_c,
                                     start=False, stop=False).then_inc(TE.h, 1)
                            TE.n += 1
                            t.matmul(prps[ts % 3][:, 0:256], xl_c, wh_c,
                                     start=False, stop=(cci == NCC - 1)
                                     ).then_inc(TE.h, 1)
                            TE.n += 1

                def f_s(s, name=name, te_b=te_b):
                    for ts in range(16):
                        s.wait_ge(TE.h, te_b + 24 * (ts + 1))
                        if name in ("q", "k"):
                            s.copy(qtm[:, ts, :], prps[ts % 3][:, 0:256]
                                   ).then_inc(SE.h, 1)
                            SE.n += 1
                        else:
                            s.copy(vth[:, 0, ts, 0:128], prps[ts % 3][:, 0:128])
                            s.copy(vth[:, 1, ts, 0:128], prps[ts % 3][:, 128:256]
                                   ).then_inc(SE.h, 1)
                            SE.n += 1

                def f_vlo(v, name=name, te_b=te_b, se_b=se_b):
                    if name != "v":
                        return
                    for ts in range(16):
                        v.wait_ge(SE.h, se_b + ts + 1)
                        v.tensor_tensor(vtl[:, 0, ts, 0:128],
                                        prps[ts % 3][:, 0:128],
                                        vth[:, 0, ts, 0:128], AL.subtract)
                        v.tensor_tensor(vtl[:, 1, ts, 0:128],
                                        prps[ts % 3][:, 128:256],
                                        vth[:, 1, ts, 0:128], AL.subtract
                                        ).then_inc(VE.h, 1)
                        VE.n += 1

                blk.vector(f_vsp)
                blk.tensor(f_t)
                blk.scalar(f_s)
                blk.vector(f_vlo)

            if name == "v":
                with nc.Block(f"avo{hg}") as blk:
                    def f_v(v):
                        v.memset(vth[:, :, :, 128:129], 1.0)
                        v.memset(vth[:, :, :, 129:132], 0.0)
                        v.memset(vtl[:, :, :, 128:132], 0.0)

                    blk.vector(f_v)
                continue

            dsti = 0 if name == "q" else 1
            qmul = 0.125 if name == "q" else None

            with nc.Block(f"arp{hg}{name}") as blk:
                def f_v(v):
                    for g4 in range(4):
                        tsl = slice(g4 * 4, g4 * 4 + 4)
                        if g4 >= 1:
                            v.drain()
                        v.tensor_tensor(rsc1[:, :, :], qtm[:, tsl, :],
                                        qtm[:, tsl, :], AL.mult)
                        v.drain()
                        v.reduce_sum(msq[:, tsl, :].rearrange(
                            "p a (b c) -> p a b c", c=1),
                            rsc1[:, :, :].rearrange("p a (b c) -> p a b c", b=4),
                            axis=AX)
                    v.drain()
                    v.tensor_scalar(msq[:, :, :], msq[:, :, :], 1.0 / HD,
                                    RMS_EPS, AL.mult, AL.add).then_inc(VE.h, 1)
                    VE.n += 1

                def f_s(s):
                    s.wait_ge(VE.h, VE.n)
                    s.activation(msq[:, :, :], msq[:, :, :], AF.Ln)
                    s.drain()
                    s.activation(msq[:, :, :], msq[:, :, :], AF.Exp, scale=-0.5
                                 ).then_inc(SE.h, 1)
                    SE.n += 1

                blk.vector(f_v)
                blk.scalar(f_s)

            with nc.Block(f"arr{hg}{name}") as blk:
                def f_v(v, qmul=qmul):
                    v.wait_ge(SE.h, SE.n)
                    if qmul is not None:
                        v.tensor_scalar(msq[:, :, :], msq[:, :, :], qmul, None,
                                        AL.mult)
                        v.drain()
                    for g4 in range(4):
                        q4 = bass.AP(qtm.tensor, g4 * 1024,
                                     [[4096, 128], [1, 1024]])
                        r1f = bass.AP(rsc1.tensor, 0, [[1024, 128], [1, 1024]])
                        r2f = bass.AP(rsc2.tensor, 0, [[1024, 128], [1, 1024]])
                        r3f = bass.AP(rsc3.tensor, 0, [[1024, 128], [1, 1024]])
                        msq_b = bass.AP(msq.tensor, g4 * 16,
                                        [[64, 128], [1, 16], [0, HD]])
                        cc_b = bass.AP(cc_all.tensor, g4 * 4 * HD,
                                       [[16 * HD, 128], [HD, 4], [0, 4], [1, HD]])
                        ss_b = bass.AP(ss_all.tensor, g4 * 4 * HD,
                                       [[16 * HD, 128], [HD, 4], [0, 4], [1, HD]])
                        r1sw = bass.AP(rsc1.tensor, 32,
                                       [[1024, 128], [64, 16], [-32, 2], [1, 32]])
                        q4v = bass.AP(qtm.tensor, g4 * 1024,
                                      [[4096, 128], [64, 16], [1, 64]])
                        r1v = bass.AP(rsc1.tensor, 0,
                                      [[1024, 128], [64, 16], [1, 64]])
                        r2v = bass.AP(rsc2.tensor, 0,
                                      [[1024, 128], [64, 16], [1, 64]])
                        r3v = bass.AP(rsc3.tensor, 0,
                                      [[1024, 128], [64, 16], [1, 64]])
                        v.tensor_tensor(r1v, q4v, msq_b, AL.mult)
                        v.drain()
                        v.tensor_tensor(r2v, r1v, cc_b, AL.mult)
                        v.tensor_tensor(r3v, r1sw, ss_b, AL.mult)
                        v.drain()
                        v.tensor_tensor(q4, r2f, r3f, AL.add)
                        v.drain()
                    v.engine_nop().then_inc(VE.h, 1)
                    VE.n += 1

                blk.vector(f_v)

            with nc.Block(f"atr{hg}{name}") as blk:
                te_b = TE.n
                ve_b = VE.n

                def f_t(t, te_b=te_b, ve_b=ve_b):
                    for grp in range(8):
                        if grp >= 1:
                            t.wait_ge(VE.h, ve_b + grp)
                        for j in range(4):
                            ts2 = grp * 2 + j // 2
                            dc = j % 2
                            t.matmul(otps[:, j * 128:j * 128 + 128],
                                     qtm[:, ts2, dc * 128:dc * 128 + 128],
                                     id128[:, :], is_transpose=True
                                     ).then_inc(TE.h, 1)
                            TE.n += 1

                def f_v(v, dsti=dsti, te_b=te_b):
                    for grp in range(8):
                        v.wait_ge(TE.h, te_b + 4 * (grp + 1))
                        for j in range(4):
                            ts2 = grp * 2 + j // 2
                            dc = j % 2
                            tsl = slice(ts2 * 128, ts2 * 128 + 128)
                            pslc = otps[:, j * 128:j * 128 + 128]
                            if dsti == 1:
                                v.tensor_copy(kpkh[:, dc, tsl], pslc)
                                v.drain()
                                v.tensor_tensor(kpkl[:, dc, tsl], pslc,
                                                kpkh[:, dc, tsl], AL.subtract)
                            else:
                                ph_ = otps[0:64, j * 128:j * 128 + 128]
                                po_ = otps[64:128, j * 128:j * 128 + 128]
                                v.tensor_copy(qEh[0:64, dc, tsl], ph_)
                                v.tensor_copy(qOh[64:128, dc, tsl], po_)
                                v.drain()
                                v.tensor_tensor(qEl[0:64, dc, tsl], ph_,
                                                qEh[0:64, dc, tsl], AL.subtract)
                                v.tensor_tensor(qOl[64:128, dc, tsl], po_,
                                                qOh[64:128, dc, tsl],
                                                AL.subtract)
                        v.engine_nop().then_inc(VE.h, 1)
                        VE.n += 1

                blk.tensor(f_t)
                blk.vector(f_v)

        for hl in range(2):
            h = hg * 2 + hl

            qt_items = []
            for qt in range(8):
                items = [(cp, kc) for cp in range(2) for kc in range(2 * qt + 2)]
                qt_items.append(items)

            with nc.Block(f"asv{hg}h{hl}") as blk:
                te_b = TE.n
                se_b = SE.n
                ve_b = VE.n
                te_after_s = {}
                te_after_av = {}
                te_group = {}
                te_tr = {}
                se_exp = {}
                se_evac = {}
                se_otcp = {}
                ve_p = {}
                ve_evac = {}
                ve_stt2 = {}
                ve_otl = {}
                tcnt, scnt, vcnt = te_b, se_b, ve_b
                for qt in range(8):
                    n = len(qt_items[qt])
                    for i in range(n + 3):
                        if i < n:
                            tcnt += 3
                            te_after_s[(qt, i)] = tcnt
                        if i >= 3:
                            tcnt += 6
                            te_after_av[(qt, i - 3)] = tcnt
                    te_group[qt] = tcnt
                    if qt >= 1:
                        tcnt += 2
                        te_tr[qt - 1] = tcnt
                tcnt += 2
                te_tr[7] = tcnt

                def _tab_A(m, scnt, vcnt):
                    ev = {}
                    sv = {}
                    vcnt += 1; ev["rcp0"] = vcnt
                    vcnt += 1; ev["rcp1"] = vcnt
                    scnt += 1; sv["t10"] = scnt
                    scnt += 1; sv["t11"] = scnt
                    vcnt += 1; ev["stt0"] = vcnt
                    vcnt += 1; ev["stt1"] = vcnt
                    ve_evac[m] = ev
                    se_evac[m] = sv
                    ve_stt2[m] = ev["stt1"]
                    return scnt, vcnt

                def _tab_B(m, scnt, vcnt):
                    ev = ve_evac[m]
                    sv = se_evac[m]
                    for sub in range(2):
                        vcnt += 1; ev["lnst%d" % sub] = vcnt
                        scnt += 1; sv["scl%d" % sub] = scnt
                        vcnt += 1; ev["bias%d" % sub] = vcnt
                        scnt += 1; sv["oln%d" % sub] = scnt
                    return scnt, vcnt

                for qt in range(8):
                    n = len(qt_items[qt])
                    for i in range(min(3, n)):
                        scnt += 1; se_exp[(qt, i)] = scnt
                        vcnt += 1; ve_p[(qt, i)] = vcnt
                    if qt >= 1:
                        scnt, vcnt = _tab_A(qt - 1, scnt, vcnt)
                    for i in range(min(3, n), min(6, n)):
                        scnt += 1; se_exp[(qt, i)] = scnt
                        vcnt += 1; ve_p[(qt, i)] = vcnt
                    if qt >= 1:
                        scnt, vcnt = _tab_B(qt - 1, scnt, vcnt)
                    if qt >= 2:
                        scnt += 1; se_otcp[qt - 2] = scnt
                        vcnt += 1; ve_otl[qt - 2] = vcnt
                    for i in range(min(6, n), n):
                        scnt += 1; se_exp[(qt, i)] = scnt
                        vcnt += 1; ve_p[(qt, i)] = vcnt
                scnt, vcnt = _tab_A(7, scnt, vcnt)
                scnt, vcnt = _tab_B(7, scnt, vcnt)
                scnt += 1; se_otcp[6] = scnt
                scnt += 1; se_otcp[7] = scnt
                vcnt += 1; ve_otl[6] = vcnt
                vcnt += 1; ve_otl[7] = vcnt
                te_total, se_total, ve_total = tcnt, scnt, vcnt
                ots_b = OTS.i

                def f_t(t, hl=hl):
                    def tr(m):
                        t.wait_ge(SE.h, se_evac[m]["oln1"])
                        if m >= 1:
                            t.wait_ge(SE.h, se_otcp[m - 1])
                            t.wait_ge(VE.h, ve_otl[m - 1])
                        t.matmul(otps[:, 0:128], olnb[:, 0:128], id128[:, :],
                                 is_transpose=True).then_inc(TE.h, 1)
                        TE.n += 1
                        t.matmul(otps[:, 128:256], olnb[:, 128:256],
                                 id128[:, :], is_transpose=True
                                 ).then_inc(TE.h, 1)
                        TE.n += 1
                        assert TE.n == te_tr[m]

                    for qt in range(8):
                        items = qt_items[qt]
                        n = len(items)
                        for i in range(n + 3):
                            if i < n:
                                cp, kc = items[i]
                                hh = 2 * hl + cp
                                if i >= 3:
                                    t.wait_ge(SE.h, se_exp[(qt, i - 3)])
                                elif qt >= 1:
                                    t.wait_ge(SE.h, se_exp[(qt - 1,
                                                            len(qt_items[qt - 1])
                                                            - 3 + i)])
                                dcq = hh // 2
                                qh_ = qEh if (hh % 2) == 0 else qOh
                                ql_ = qEl if (hh % 2) == 0 else qOl
                                ksl = slice(kc * 128, kc * 128 + 128)
                                qsl = slice(qt * 256, qt * 256 + 256)
                                t.matmul(sps[i % 3][:, 0:256],
                                         kpkh[:, dcq, ksl], qh_[:, dcq, qsl],
                                         start=True, stop=False).then_inc(TE.h, 1)
                                TE.n += 1
                                t.matmul(sps[i % 3][:, 0:256],
                                         kpkh[:, dcq, ksl], ql_[:, dcq, qsl],
                                         start=False, stop=False).then_inc(TE.h, 1)
                                TE.n += 1
                                t.matmul(sps[i % 3][:, 0:256],
                                         kpkl[:, dcq, ksl], qh_[:, dcq, qsl],
                                         start=False, stop=True).then_inc(TE.h, 1)
                                TE.n += 1
                                assert TE.n == te_after_s[(qt, i)]
                            if i >= 3:
                                cp, kc = items[i - 3]
                                t.wait_ge(VE.h, ve_p[(qt, i - 3)])
                                if kc == 0 and qt >= 1:
                                    if cp == 0:
                                        t.wait_ge(VE.h,
                                                  ve_evac[qt - 1]["rcp1"])
                                        t.wait_ge(SE.h,
                                                  se_evac[qt - 1]["t11"])
                                    else:
                                        t.wait_ge(VE.h, ve_stt2[qt - 1])
                                for sub in range(2):
                                    ssl = slice(sub * 128, sub * 128 + 128)
                                    pav = avps[cp * 2 + sub]
                                    t.matmul(pav[:, 0:132],
                                             pbh[(i - 3) % 4][:, ssl],
                                             vth[:, hl, kc, 0:132],
                                             start=(kc == 0), stop=False
                                             ).then_inc(TE.h, 1)
                                    TE.n += 1
                                    t.matmul(pav[:, 0:132],
                                             pbh[(i - 3) % 4][:, ssl],
                                             vtl[:, hl, kc, 0:132],
                                             start=False, stop=False
                                             ).then_inc(TE.h, 1)
                                    TE.n += 1
                                    t.matmul(pav[:, 0:132],
                                             pbl[(i - 3) % 4][:, ssl],
                                             vth[:, hl, kc, 0:132],
                                             start=False,
                                             stop=(kc == 2 * qt + 1)
                                             ).then_inc(TE.h, 1)
                                    TE.n += 1
                        assert TE.n == te_group[qt]
                        if qt >= 1:
                            tr(qt - 1)
                    tr(7)

                def f_s(s, hl=hl):
                    def otcp(m):
                        s.wait_ge(TE.h, te_tr[m])
                        tok0 = ots_b + 2 * (m - 2)
                        if tok0 >= 0:
                            OTS.wait(s, tok0)
                            OTS.wait(s, tok0 + 1)
                        s.copy(oth[:, m % 2, :], otps[:, 0:256]
                               ).then_inc(SE.h, 1)
                        SE.n += 1
                        assert SE.n == se_otcp[m]

                    def esA(m):
                        ev = ve_evac[m]
                        sv = se_evac[m]
                        for sub in range(2):
                            ob = 8 * sub
                            s.wait_ge(VE.h, ev["rcp%d" % sub])
                            s.activation(oc1[:, sub, :], avps[sub][:, 0:128],
                                         AF.Identity,
                                         scale=occ[:, ob + 0:ob + 1]
                                         ).then_inc(SE.h, 1)
                            SE.n += 1
                            assert SE.n == sv["t1%d" % sub]

                    def esB(m):
                        ev = ve_evac[m]
                        sv = se_evac[m]
                        for sub in range(2):
                            ob = 8 * sub
                            s.wait_ge(VE.h, ev["lnst%d" % sub])
                            s.activation(occ[:, ob + 6:ob + 7],
                                         occ[:, ob + 5:ob + 6], AF.Ln)
                            s.drain()
                            s.activation(occ[:, ob + 6:ob + 7],
                                         occ[:, ob + 6:ob + 7], AF.Exp,
                                         scale=-0.5).then_inc(SE.h, 1)
                            SE.n += 1
                            assert SE.n == sv["scl%d" % sub]
                            s.wait_ge(VE.h, ev["bias%d" % sub])
                            if sub == 0 and m >= 1:
                                s.wait_ge(TE.h, te_tr[m - 1])
                            s.activation(olnb[:, sub * 128:sub * 128 + 128],
                                         oc2[:, sub, :], AF.Identity,
                                         scale=occ[:, ob + 6:ob + 7],
                                         bias=occ[:, ob + 7:ob + 8]
                                         ).then_inc(SE.h, 1)
                            SE.n += 1
                            assert SE.n == sv["oln%d" % sub]

                    def expi(qt, i):
                        cp, kc = qt_items[qt][i]
                        s.wait_ge(TE.h, te_after_s[(qt, i)])
                        if i >= 4:
                            s.wait_ge(TE.h, te_after_av[(qt, i - 4)])
                        elif qt >= 1:
                            s.wait_ge(TE.h,
                                      te_after_av[(qt - 1,
                                                   len(qt_items[qt - 1])
                                                   - 4 + i)])
                        if kc >= 2 * qt:
                            s.activation(p32[i % 4][:, :], sps[i % 3][:, 0:256],
                                         AF.Exp).then_inc(SE.h, 1)
                        else:
                            s.activation(p32[i % 4][:, :], sps[i % 3][:, 0:256],
                                         AF.Exp)
                            s.activation(pbh[i % 4][:, :], sps[i % 3][:, 0:256],
                                         AF.Exp).then_inc(SE.h, 1)
                        SE.n += 1
                        assert SE.n == se_exp[(qt, i)]

                    for qt in range(8):
                        n = len(qt_items[qt])
                        for i in range(min(3, n)):
                            expi(qt, i)
                        if qt >= 1:
                            esA(qt - 1)
                        for i in range(min(3, n), min(6, n)):
                            expi(qt, i)
                        if qt >= 1:
                            esB(qt - 1)
                        if qt >= 2:
                            otcp(qt - 2)
                        for i in range(min(6, n), n):
                            expi(qt, i)
                    esA(7)
                    esB(7)
                    otcp(6)
                    otcp(7)

                def f_v(v, hl=hl):
                    def otlf(m):
                        v.wait_ge(SE.h, se_otcp[m])
                        v.tensor_tensor(otl[:, m % 2, :], otps[:, 0:256],
                                        oth[:, m % 2, :], AL.subtract
                                        ).then_inc(VE.h, 1)
                        VE.n += 1
                        assert VE.n == ve_otl[m]

                    def evA(m):
                        ev = ve_evac[m]
                        sv = se_evac[m]
                        v.wait_ge(TE.h, te_group[m])
                        for sub in range(2):
                            ob = 8 * sub
                            v.reciprocal(occ[:, ob + 0:ob + 1],
                                         avps[sub][:, 128:129])
                            v.reciprocal(occ[:, ob + 1:ob + 2],
                                         avps[2 + sub][:, 128:129])
                            v.drain()
                            v.tensor_tensor(occ[:, ob + 1:ob + 2],
                                            occ[:, ob + 1:ob + 2], lamneg[:, :],
                                            AL.mult).then_inc(VE.h, 1)
                            VE.n += 1
                            assert VE.n == ev["rcp%d" % sub]
                        for sub in range(2):
                            ob = 8 * sub
                            v.wait_ge(SE.h, sv["t1%d" % sub])
                            v.scalar_tensor_tensor(oc2[:, sub, :],
                                                   avps[2 + sub][:, 0:128],
                                                   occ[:, ob + 1:ob + 2],
                                                   oc1[:, sub, :], AL.mult,
                                                   AL.add).then_inc(VE.h, 1)
                            VE.n += 1
                            assert VE.n == ev["stt%d" % sub]

                    def evB(m):
                        ev = ve_evac[m]
                        sv = se_evac[m]
                        for sub in range(2):
                            ob = 8 * sub
                            v.drain()
                            v.reduce_sum(occ[:, ob + 2:ob + 3],
                                         oc2[:, sub, :].rearrange(
                                             "p (a b) -> p a b", a=1), axis=AX)
                            v.tensor_tensor(oc1[:, sub, :], oc2[:, sub, :],
                                            oc2[:, sub, :], AL.mult)
                            v.drain()
                            v.reduce_sum(occ[:, ob + 3:ob + 4],
                                         oc1[:, sub, :].rearrange(
                                             "p (a b) -> p a b", a=1), axis=AX)
                            v.tensor_scalar(occ[:, ob + 2:ob + 3],
                                            occ[:, ob + 2:ob + 3], 1.0 / 128,
                                            None, AL.mult)
                            v.drain()
                            v.tensor_scalar(occ[:, ob + 3:ob + 4],
                                            occ[:, ob + 3:ob + 4], 1.0 / 128,
                                            None, AL.mult)
                            v.tensor_tensor(occ[:, ob + 4:ob + 5],
                                            occ[:, ob + 2:ob + 3],
                                            occ[:, ob + 2:ob + 3], AL.mult)
                            v.drain()
                            v.tensor_tensor(occ[:, ob + 5:ob + 6],
                                            occ[:, ob + 3:ob + 4],
                                            occ[:, ob + 4:ob + 5], AL.subtract)
                            v.drain()
                            v.tensor_scalar(occ[:, ob + 5:ob + 6],
                                            occ[:, ob + 5:ob + 6], 1.0, LN_EPS,
                                            AL.mult, AL.add).then_inc(VE.h, 1)
                            VE.n += 1
                            assert VE.n == ev["lnst%d" % sub]
                            v.wait_ge(SE.h, sv["scl%d" % sub])
                            v.tensor_scalar(occ[:, ob + 6:ob + 7],
                                            occ[:, ob + 6:ob + 7], ONE_MINUS_LI,
                                            None, AL.mult)
                            v.drain()
                            v.scalar_tensor_tensor(occ[:, ob + 7:ob + 8],
                                                   occ[:, ob + 2:ob + 3], -1.0,
                                                   occ[:, ob + 6:ob + 7],
                                                   AL.mult, AL.mult
                                                   ).then_inc(VE.h, 1)
                            VE.n += 1
                            assert VE.n == ev["bias%d" % sub]

                    def spliti(qt, i):
                        cp, kc = qt_items[qt][i]
                        v.wait_ge(SE.h, se_exp[(qt, i)])
                        if i >= 4:
                            v.wait_ge(TE.h, te_after_av[(qt, i - 4)])
                        elif qt >= 1:
                            v.wait_ge(TE.h,
                                      te_after_av[(qt - 1,
                                                   len(qt_items[qt - 1])
                                                   - 4 + i)])
                        if kc >= 2 * qt:
                            msk = m0 if kc == 2 * qt else m1
                            v.tensor_tensor(p32[i % 4][:, :],
                                            p32[i % 4][:, :],
                                            msk[:, :], AL.mult)
                            v.drain()
                            v.tensor_copy(pbh[i % 4][:, :], p32[i % 4][:, :])
                            v.drain()
                        v.tensor_tensor(pbl[i % 4][:, :], p32[i % 4][:, :],
                                        pbh[i % 4][:, :], AL.subtract
                                        ).then_inc(VE.h, 1)
                        VE.n += 1
                        assert VE.n == ve_p[(qt, i)]

                    for qt in range(8):
                        n = len(qt_items[qt])
                        for i in range(min(3, n)):
                            spliti(qt, i)
                        if qt >= 1:
                            evA(qt - 1)
                        for i in range(min(3, n), min(6, n)):
                            spliti(qt, i)
                        if qt >= 1:
                            evB(qt - 1)
                        if qt >= 2:
                            otlf(qt - 2)
                        for i in range(min(6, n), n):
                            spliti(qt, i)
                    evA(7)
                    evB(7)
                    otlf(6)
                    otlf(7)

                def f_sy(sy, h=h):
                    for m in range(8):
                        sy.wait_ge(SE.h, se_otcp[m])
                        sy.wait_ge(VE.h, ve_otl[m])
                        OTS.issue(sy, otd[h][0][:, m * 256:m * 256 + 256],
                                  oth[:, m % 2, :])
                        OTS.issue(sy, otd[h][1][:, m * 256:m * 256 + 256],
                                  otl[:, m % 2, :])

                blk.tensor(f_t)
                blk.scalar(f_s)
                blk.vector(f_v)
                blk.sync(f_sy)
                assert TE.n == te_total and SE.n == se_total and VE.n == ve_total

    ax.close()
    if _PHASE_LIMIT <= 5:
        ctx.close()
        return nc


    px = ExitStack()
    wpsb = sb("wpsb", [128, NCC, C], cx=px)
    wpsbb = sb("wpsbb", [128, 2, NCC, C], BF16, cx=px)
    ybuf = sb("ybufp", [128, 4, 512], cx=px)
    otb = [sb(f"otb{i}", [128, 2, NCC, 512], BF16, cx=px) for i in range(2)]
    ypps = [ps(f"ypps{i}", [128, 512], cx=px) for i in range(2)]

    with nc.Block("pw0") as blk:
        def f_sy(sy):
            OTS.wait_all(sy)
            sy.dma_start(wpsb[:, :, :],
                         bass.AP(wpT.tensor, 0, [[C, 128], [128 * C, NCC], [1, C]])
                         ).then_inc(LD.h, 16)
            LD.n += 16

        def f_v(v):
            ADS.wait_all(v)
            v.wait_ge(LD.h, LD.n)
            v.tensor_copy(wpsbb[:, 0, :, :], wpsb[:, :, :])
            v.drain()
            v.tensor_tensor(wpsbb[:, 1, :, :], wpsb[:, :, :],
                            wpsbb[:, 0, :, :], AL.subtract).then_inc(VE.h, 1)
            VE.n += 1

        blk.sync(f_sy)
        blk.vector(f_v)

    if _PHASE_LIMIT <= 6:
        px.close()
        ctx.close()
        return nc

    with nc.Block("pw1") as blk:
        te_b = TE.n
        se_b = SE.n
        ve_b = VE.n
        ads_b = ADS.i
        obs_b = OBS.i

        def f_sy(sy, te_b=te_b):
            for tt in range(4):
                if tt >= 2:
                    sy.wait_ge(TE.h, te_b + 192 * (tt - 1))
                for hl_ in range(2):
                    OBS.issue(sy, otb[tt % 2][:, hl_, :, :],
                              bass.AP(otd.tensor, hl_ * 128 * T + tt * 512,
                                      [[T, 128], [2 * 128 * T, NCC], [1, 512]]))

        def f_t(t, se_b=se_b, obs_b=obs_b, ve_b=ve_b):
            t.wait_ge(VE.h, ve_b)
            for tt in range(4):
                OBS.wait(t, obs_b + 2 * tt)
                OBS.wait(t, obs_b + 2 * tt + 1)
                for co in range(NCC):
                    j = tt * NCC + co
                    if j >= 2:
                        t.wait_ge(SE.h, se_b + j - 1)
                    for hh in range(NCC):
                        wh_ = wpsbb[:, 0, hh, co * 128:co * 128 + 128]
                        wl_ = wpsbb[:, 1, hh, co * 128:co * 128 + 128]
                        if os.environ.get("KSUB") == "2":
                            t.matmul(ypps[j % 2][:, :], wh_,
                                     otb[tt % 2][:, 0, hh, :], start=(hh == 0),
                                     stop=(hh == NCC - 1)).then_inc(TE.h, 3)
                            TE.n += 3
                            continue
                        t.matmul(ypps[j % 2][:, :], wh_,
                                 otb[tt % 2][:, 0, hh, :], start=(hh == 0),
                                 stop=False).then_inc(TE.h, 1)
                        TE.n += 1
                        t.matmul(ypps[j % 2][:, :], wh_,
                                 otb[tt % 2][:, 1, hh, :], start=False,
                                 stop=False).then_inc(TE.h, 1)
                        TE.n += 1
                        t.matmul(ypps[j % 2][:, :], wl_,
                                 otb[tt % 2][:, 0, hh, :], start=False,
                                 stop=(hh == NCC - 1)).then_inc(TE.h, 1)
                        TE.n += 1

        def f_s(s, te_b=te_b, ads_b=ads_b):
            for j in range(32):
                s.wait_ge(TE.h, te_b + 24 * (j + 1))
                if ads_b + j >= 4:
                    ADS.wait(s, ads_b + j - 4)
                s.copy(ybuf[:, (ads_b + j) % 4, :], ypps[j % 2][:, :]
                       ).then_inc(SE.h, 1)
                SE.n += 1

        def f_g(g, se_b=se_b, ads_b=ads_b):
            ADS.wait_all(g)
            for tt in range(4):
                for co in range(NCC):
                    j = tt * NCC + co
                    g.wait_ge(SE.h, se_b + j + 1)
                    if os.environ.get("KSUB") == "1":
                        ADS.issue(g, out[co * 128:co * 128 + 128,
                                         tt * 512:tt * 512 + 512],
                                  ybuf[:, (ads_b + j) % 4, :])
                    else:
                        ADS.issue(g, out[co * 128:co * 128 + 128,
                                         tt * 512:tt * 512 + 512],
                                  ybuf[:, (ads_b + j) % 4, :], accum_op=AL.add)

        blk.sync(f_sy)
        blk.tensor(f_t)
        blk.scalar(f_s)
        blk.gpsimd(f_g)

    px.close()

    if _PHASE_LIMIT <= 7:
        ctx.close()
        return nc

    with nc.Block("fin") as blk:
        def f_g(g):
            ADS.wait_all(g)

        def f_sy(sy):
            sy.wait_ge(LD.h, LD.n)
            for st in (XTS, WFS, DDS, WAS, OTS, OBS, SQS):
                st.wait_all(sy)

        blk.gpsimd(f_g)
        blk.sync(f_sy)

    ctx.close()
    return nc


_CACHE = {}


def _prep_inputs(inputs):
    x = np.asarray(inputs["L0_attn_x"], dtype=np.float32)
    w = {}
    w["wqT"] = np.ascontiguousarray(np.asarray(inputs["L0_attn_Wq"]).T)
    w["wkT"] = np.ascontiguousarray(np.asarray(inputs["L0_attn_Wk"]).T)
    w["wvT"] = np.ascontiguousarray(np.asarray(inputs["L0_attn_Wv"]).T)
    w["wpT"] = np.ascontiguousarray(np.asarray(inputs["L0_attn_Wproj"]).T)
    w["gwTd"] = np.ascontiguousarray(np.asarray(inputs["L0_ffn_gate_w"]).T)
    w["lbd"] = np.asarray(inputs["L0_ffn_lb_bias"]).reshape(1, E)
    w["egwT"] = np.ascontiguousarray(
        np.asarray(inputs["L0_ffn_expert_gate_w"]).transpose(0, 2, 1))
    w["euwT"] = np.ascontiguousarray(
        np.asarray(inputs["L0_ffn_expert_up_w"]).transpose(0, 2, 1))
    w["edwT"] = np.ascontiguousarray(
        np.asarray(inputs["L0_ffn_expert_down_w"]).transpose(0, 2, 1))
    w["sgwT"] = np.ascontiguousarray(np.asarray(inputs["L0_ffn_shared_gate_w"]).T)
    w["suwT"] = np.ascontiguousarray(np.asarray(inputs["L0_ffn_shared_up_w"]).T)
    w["sdwT"] = np.ascontiguousarray(np.asarray(inputs["L0_ffn_shared_down_w"]).T)
    inv_freq = 1.0 / 10000.0 ** (np.arange(0, HD, 2, dtype=np.float32) / HD)
    tpos = np.arange(T, dtype=np.float32)
    freqs = np.outer(tpos, inv_freq)
    cosf = np.cos(freqs)
    sinf = np.sin(freqs)
    ct = np.zeros((128, 16, 2, 32), np.float32)
    st = np.zeros((128, 16, 2, 32), np.float32)
    for half in range(2):
        ct[:, :, half, :] = cosf.reshape(16, 128, 32).transpose(1, 0, 2)
        sgn = 1.0 if half == 0 else -1.0
        st[:, :, half, :] = sgn * sinf.reshape(16, 128, 32).transpose(1, 0, 2)
    w["cosd"] = np.ascontiguousarray(ct.reshape(128, 1024))
    w["sind"] = np.ascontiguousarray(st.reshape(128, 1024))
    w["lqk_d"] = np.concatenate([
        np.asarray(inputs["L0_attn_lq1"]), np.asarray(inputs["L0_attn_lk1"]),
        np.asarray(inputs["L0_attn_lq2"]), np.asarray(inputs["L0_attn_lk2"]),
    ]).reshape(1, 4 * HD).astype(np.float32)

    in_maps = []
    for b in range(8):
        rolled = np.concatenate([x[b:], x[:b]], axis=0)
        xTb = np.ascontiguousarray(rolled.reshape(TG, C).T)
        m = {"xT": xTb}
        m.update(w)
        in_maps.append(m)
    return in_maps


def kernel(**inputs):
    if "nc" not in _CACHE:
        _CACHE["nc"] = build_program()
    nc = _CACHE["nc"]
    in_maps = _prep_inputs(inputs)
    res = run_bass_kernel_spmd(nc, in_maps, core_ids=list(range(8)))
    outp = np.stack([np.asarray(res.results[b]["out"]).T for b in range(8)])
    return outp.astype(np.float32)


# version 13 (final)
# speedup vs baseline: 1.0545x; 1.0545x over previous
import math
import os
from contextlib import ExitStack

_PHASE_LIMIT = int(os.environ.get("KPHASE", "99"))

import numpy as np

import concourse.bass as bass
import concourse.mybir as mybir
from concourse.alu_op_type import AluOpType as AL
from concourse.bass_utils import run_bass_kernel_spmd

F32 = mybir.dt.float32
BF16 = mybir.dt.bfloat16
I32 = mybir.dt.int32
AF = mybir.ActivationFunctionType
AX = mybir.AxisListType.X

C = 1024
T = 2048
TG = 16384
NCC = 8
NCHG = TG // 128
NCHL = T // 128
HD = 64
E = 8
IEXP = 512
LAMBDA_INIT = 0.8 - 0.6 * math.exp(-0.3 * 4)
ONE_MINUS_LI = 1.0 - LAMBDA_INIT
RMS_EPS = float(np.finfo(np.float32).eps)
LN_EPS = 1e-5
HUGE = 1.0e15
INVC = 1.0 / C
INV2PI = 1.0 / (2.0 * math.pi)
TWOPI = 2.0 * math.pi


class Ctr:

    def __init__(self, handle):
        self.h = handle
        self.n = 0

    def inc(self, instr, k=1):
        instr.then_inc(self.h, k)
        self.n += k
        return self.n


class DmaStream:

    def __init__(self, nc, ctx, name, depth):
        self.depth = depth
        self.sems = [Ctr(ctx.enter_context(nc.semaphore(f"{name}{i}")))
                     for i in range(depth)]
        self.i = 0

    def issue(self, eng, out_ap, in_ap, **kw):
        tok = self.i
        sc = self.sems[tok % self.depth]
        if tok >= self.depth:
            eng.wait_ge(sc.h, 16 * (tok // self.depth))
        eng.dma_start(out_ap, in_ap, **kw).then_inc(sc.h, 16)
        sc.n += 16
        self.i += 1
        return tok

    def wait(self, eng, tok):
        sc = self.sems[tok % self.depth]
        eng.wait_ge(sc.h, 16 * (tok // self.depth + 1))

    def wait_all(self, eng):
        for sc in self.sems:
            eng.wait_ge(sc.h, sc.n)


def build_program():
    nc = bass.Bass("TRN2", target_bir_lowering=False)

    xT = nc.dram_tensor("xT", [C, TG], F32, kind="ExternalInput").ap()
    wqT = nc.dram_tensor("wqT", [C, C], F32, kind="ExternalInput").ap()
    wkT = nc.dram_tensor("wkT", [C, C], F32, kind="ExternalInput").ap()
    wvT = nc.dram_tensor("wvT", [C, C], F32, kind="ExternalInput").ap()
    wpT = nc.dram_tensor("wpT", [C, C], F32, kind="ExternalInput").ap()
    gwTd = nc.dram_tensor("gwTd", [C, E], F32, kind="ExternalInput").ap()
    lbd = nc.dram_tensor("lbd", [1, E], F32, kind="ExternalInput").ap()
    egwT = nc.dram_tensor("egwT", [E, C, IEXP], F32, kind="ExternalInput").ap()
    euwT = nc.dram_tensor("euwT", [E, C, IEXP], F32, kind="ExternalInput").ap()
    edwT = nc.dram_tensor("edwT", [E, IEXP, C], F32, kind="ExternalInput").ap()
    sgwT = nc.dram_tensor("sgwT", [C, 2 * IEXP], F32, kind="ExternalInput").ap()
    suwT = nc.dram_tensor("suwT", [C, 2 * IEXP], F32, kind="ExternalInput").ap()
    sdwT = nc.dram_tensor("sdwT", [2 * IEXP, C], F32, kind="ExternalInput").ap()
    lqk_d = nc.dram_tensor("lqk_d", [1, 4 * HD], F32, kind="ExternalInput").ap()
    cosd = nc.dram_tensor("cosd", [128, 16 * HD], F32, kind="ExternalInput").ap()
    sind = nc.dram_tensor("sind", [128, 16 * HD], F32, kind="ExternalInput").ap()
    out = nc.dram_tensor("out", [C, T], F32, kind="ExternalOutput").ap()
    dbg = nc.dram_tensor("dbg", [128, T], BF16,
                         kind="ExternalOutput" if os.environ.get("KDBG") else "Internal"
                         ).ap()
    otd = nc.dram_tensor("otd", [8, 2, 128, T], BF16,
                         kind="ExternalOutput" if os.environ.get("KDBG") else "Internal"
                         ).ap()

    ctx = ExitStack()

    def sb(name, shape, dt=F32, cx=None):
        return (cx or ctx).enter_context(nc.sbuf_tensor(name, shape, dt)).ap()

    def ps(name, shape, cx=None, dt=F32):
        return (cx or ctx).enter_context(nc.psum_tensor(name, shape, dt)).ap()

    LD = Ctr(ctx.enter_context(nc.semaphore("LD")))
    TE = Ctr(ctx.enter_context(nc.semaphore("TE")))
    VE = Ctr(ctx.enter_context(nc.semaphore("VE")))
    SE = Ctr(ctx.enter_context(nc.semaphore("SE")))
    ADS = DmaStream(nc, ctx, "AD", 4)
    SQS = DmaStream(nc, ctx, "SQ", 2)
    XTS = DmaStream(nc, ctx, "XT", 2)
    WFS = DmaStream(nc, ctx, "WF", 2)
    DDS = DmaStream(nc, ctx, "DD", 2)
    WAS = DmaStream(nc, ctx, "WA", 2)
    OTS = DmaStream(nc, ctx, "OT", 2)
    OBS = DmaStream(nc, ctx, "OB", 2)

    xnh = sb("xnh", [128, NCC, T], BF16)
    xnl = sb("xnl", [128, NCC, T], BF16)
    cc_all = sb("cc_all", [128, 16 * HD])
    ss_all = sb("ss_all", [128, 16 * HD])
    m0 = sb("m0", [128, 256])
    m1 = sb("m1", [128, 256])
    id128 = sb("id128", [128, 128])
    idb16 = sb("idb16", [128, 128], BF16)
    ones_row = sb("ones_row", [1, 128])
    ones_col = sb("ones_col", [128, 1])
    ones_colb = sb("ones_colb", [128, 1], BF16)
    e8t = sb("e8t", [128, 8])
    r8t = sb("r8t", [128, 8])
    lamneg = sb("lamneg", [128, 1])
    sc = sb("sc", [128, NCHL, E])
    gstat = sb("gstat", [128, NCC, 9], BF16)
    gw32 = sb("gw32", [128, NCC, E])
    cs8 = sb("cs8", [128, E])
    b8 = sb("b8", [128, E])
    wm8 = sb("wm8", [128, E])
    lqk = sb("lqk", [1, 4 * HD])
    lscr = sb("lscr", [1, 2 * HD])
    lsum = sb("lsum", [1, 2])
    lexp = sb("lexp", [1, 2])
    lamv = sb("lamv", [1, 1])
    cpool = sb("cpool", [128, 8])
    _CONSTS = [LN_EPS, 1.0]
    for k, val in enumerate(_CONSTS):
        nc.const_aps.register(F32, val, cpool[:, k:k + 1])

    sux = ExitStack()
    iota_f = sb("iota_f", [128, 256], cx=sux)
    iota_p = sb("iota_p", [128, 256], cx=sux)
    with nc.Block("su0") as blk:
        def f_g(g):
            g.iota(iota_f[:, :], [[1, 256]], channel_multiplier=0,
                   allow_small_or_imprecise_dtypes=True)
            g.iota(iota_p[:, :], [[0, 256]], channel_multiplier=1,
                   allow_small_or_imprecise_dtypes=True)
            g.iota(e8t[:, :], [[1, 8]], channel_multiplier=0,
                   allow_small_or_imprecise_dtypes=True)


        def f_sy(sy):
            sy.dma_start(gw32[:, :, :],
                         bass.AP(gwTd.tensor, 0, [[E, 128], [128 * E, NCC], [1, E]])
                         ).then_inc(LD.h, 16)
            LD.n += 16
            sy.dma_start(b8[0:1, :], lbd[:, :]).then_inc(LD.h, 16)
            LD.n += 16
            sy.dma_start(lqk[:, :], lqk_d[:, :]).then_inc(LD.h, 16)
            LD.n += 16
            sy.dma_start(cc_all[:, :], cosd[:, :]).then_inc(LD.h, 16)
            LD.n += 16
            sy.dma_start(ss_all[:, :], sind[:, :]).then_inc(LD.h, 16)
            LD.n += 16

        blk.gpsimd(f_g)
        blk.sync(f_sy)

    with nc.Block("su1") as blk:
        def f_v(v):
            v.wait_ge(LD.h, LD.n)
            v.tensor_tensor(m0[:, :], iota_p[:, :], iota_f[:, :], AL.is_le)
            v.tensor_scalar(m1[:, :], iota_p[:, :], 1.0, 128.0, AL.mult, AL.add)
            v.drain()
            v.tensor_tensor(m1[:, :], m1[:, :], iota_f[:, :], AL.is_le)
            v.tensor_tensor(id128[:, :], iota_f[:, 0:128], iota_p[:, 0:128],
                            AL.is_equal)
            v.drain()
            v.tensor_copy(idb16[:, :], id128[:, :])
            v.tensor_scalar(r8t[:, :], e8t[:, :], -1.0, 8.0, AL.mult, AL.add)
            v.memset(ones_row[:, :], 1.0)
            v.memset(ones_col[:, :], 1.0)
            v.memset(ones_colb[:, :], 1.0)
            for k, val in enumerate(_CONSTS):
                v.memset(cpool[:, k:k + 1], val)
            v.tensor_tensor(lscr[0:1, 0:HD], lqk[0:1, 0:HD],
                            lqk[0:1, HD:2 * HD], AL.mult)
            v.tensor_tensor(lscr[0:1, HD:2 * HD], lqk[0:1, 2 * HD:3 * HD],
                            lqk[0:1, 3 * HD:4 * HD], AL.mult)
            v.drain()
            v.reduce_sum(lsum[0:1, :].rearrange("a (b c) -> a b c", c=1),
                         lscr[0:1, :].rearrange("a (b c) -> a b c", b=2), axis=AX)

        blk.vector(f_v)

    with nc.Block("su2") as blk:
        def f_s(s):
            s.activation(lexp[0:1, :], lsum[0:1, :], AF.Exp)

        blk.scalar(f_s)

    with nc.Block("su3") as blk:
        def f_v(v):
            v.tensor_tensor(lamv[0:1, 0:1], lexp[0:1, 0:1], lexp[0:1, 1:2],
                            AL.subtract)
            v.drain()
            v.tensor_scalar(lamv[0:1, 0:1], lamv[0:1, 0:1], 1.0, LAMBDA_INIT,
                            AL.mult, AL.add)

        blk.vector(f_v)

    lam_ps = ps("lam_ps", [128, 512], cx=sux)
    with nc.Block("su4") as blk:
        def f_t(t):
            t.matmul(lam_ps[:, 0:1], ones_row[:, :], lamv[0:1, 0:1],
                     start=True, stop=True).then_inc(TE.h, 1)
            TE.n += 1

        def f_v(v):
            v.wait_ge(TE.h, TE.n)
            v.tensor_scalar(lamneg[:, :], lam_ps[:, 0:1], -1.0, None, AL.mult)

        blk.tensor(f_t)
        blk.vector(f_v)

    with nc.Block("su6") as blk:
        def f_s(s):
            s.copy(gstat[:, :, 0:8], gw32[:, :, :])

        blk.scalar(f_s)

    with nc.Block("su7") as blk:
        def f_v(v):
            v.memset(bass.AP(gstat.tensor, 8, [[NCC * 9, 128], [9, NCC], [1, 1]]),
                     1.0)

        blk.vector(f_v)

    sux.close()
    if _PHASE_LIMIT <= 1:
        ctx.close()
        return nc


    lnx = ExitStack()
    xloc = sb("xloc", [128, NCC, T], cx=lnx)
    xsq = sb("xsq", [128, 2, 512], cx=lnx)
    strow = sb("strow", [1, 3 * T], cx=lnx)
    MU, V2, MR = 0, 1, 2
    rbb = sb("rbb", [128, T], cx=lnx)
    mbb = sb("mbb", [128, T], cx=lnx)
    xscr = sb("xscr", [128, T], cx=lnx)
    lnp = ExitStack()
    psA = ps("psA", [1, T], cx=lnp)
    psB = ps("psB", [1, T], cx=lnp)

    with nc.Block("ln0") as blk:
        ld_b = LD.n

        def f_sy(sy):
            for cci in range(NCC):
                sy.dma_start(xloc[:, cci, :],
                             bass.AP(xT.tensor, cci * 128 * TG, [[TG, 128], [1, T]])
                             ).then_inc(LD.h, 16)
                LD.n += 16

        def f_g(g):
            ADS.issue(g, out[:, :], bass.AP(xT.tensor, 0, [[TG, C], [1, T]]))

        blk.sync(f_sy)
        blk.gpsimd(f_g)

    with nc.Block("ln1") as blk:
        ld_b2 = LD.n
        te_b = TE.n
        se_b = SE.n

        def f_s(s):
            s.wait_ge(LD.h, ld_b2)
            for cci in range(NCC):
                for mbi in range(4):
                    j = cci * 4 + mbi
                    sl = slice(mbi * 512, mbi * 512 + 512)
                    if j >= 2:
                        s.wait_ge(TE.h, te_b + 2 * (j - 1))
                    s.activation(xsq[:, j % 2, :], xloc[:, cci, sl],
                                 AF.Square).then_inc(SE.h, 1)
                    SE.n += 1

        def f_t(t):
            t.wait_ge(LD.h, ld_b2)
            for cci in range(NCC):
                for mbi in range(4):
                    j = cci * 4 + mbi
                    sl = slice(mbi * 512, mbi * 512 + 512)
                    t.wait_ge(SE.h, se_b + j + 1)
                    t.matmul(psA[:, sl], ones_col[:, :], xloc[:, cci, sl],
                             start=(cci == 0), stop=(cci == NCC - 1)
                             ).then_inc(TE.h, 1)
                    TE.n += 1
                    t.matmul(psB[:, sl], ones_col[:, :], xsq[:, j % 2, :],
                             start=(cci == 0), stop=(cci == NCC - 1)
                             ).then_inc(TE.h, 1)
                    TE.n += 1

        blk.scalar(f_s)
        blk.tensor(f_t)

    def srw(k):
        return strow[0:1, k * T:k * T + T]

    with nc.Block("ln2") as blk:
        def f_v(v):
            v.tensor_scalar(srw(MU), psA[:, :], INVC, None, AL.mult)
            v.tensor_scalar(srw(V2), psB[:, :], INVC, None, AL.mult)
            v.drain()
            v.tensor_tensor(srw(MR), srw(MU), srw(MU), AL.mult)
            v.drain()
            v.tensor_tensor(srw(V2), srw(V2), srw(MR), AL.subtract
                            ).then_inc(VE.h, 1)
            VE.n += 1

        def f_s(s):
            s.wait_ge(VE.h, VE.n)
            s.activation(srw(MR), srw(V2), AF.Sqrt, bias=LN_EPS).then_inc(SE.h, 1)
            SE.n += 1

        blk.vector(f_v)
        blk.scalar(f_s)

    with nc.Block("ln3") as blk:
        def f_v(v):
            v.wait_ge(SE.h, SE.n)
            v.reciprocal(srw(V2), srw(MR))
            v.drain()
            v.tensor_tensor(srw(MR), srw(MU), srw(V2), AL.mult).then_inc(VE.h, 1)
            VE.n += 1

        blk.vector(f_v)

    lnp.close()
    bbps = ps("bbps", [128, 512], cx=lnx)
    with nc.Block("ln4") as blk:
        te_b = TE.n
        ve_b = VE.n

        def f_t(t):
            t.wait_ge(VE.h, ve_b)
            for mbi in range(4):
                if mbi >= 1:
                    t.wait_ge(VE.h, ve_b + 2 * mbi)
                t.matmul(bbps[:, :], ones_row[:, :],
                         strow[0:1, V2 * T + mbi * 512:V2 * T + mbi * 512 + 512],
                         start=True, stop=True).then_inc(TE.h, 1)
                TE.n += 1
                t.wait_ge(VE.h, ve_b + 2 * mbi + 1)
                t.matmul(bbps[:, :], ones_row[:, :],
                         strow[0:1, MR * T + mbi * 512:MR * T + mbi * 512 + 512],
                         start=True, stop=True).then_inc(TE.h, 1)
                TE.n += 1

        def f_v(v):
            for mbi in range(4):
                sl = slice(mbi * 512, mbi * 512 + 512)
                v.wait_ge(TE.h, te_b + 2 * mbi + 1)
                v.tensor_copy(rbb[:, sl], bbps[:, :]).then_inc(VE.h, 1)
                VE.n += 1
                v.wait_ge(TE.h, te_b + 2 * mbi + 2)
                v.tensor_copy(mbb[:, sl], bbps[:, :]).then_inc(VE.h, 1)
                VE.n += 1

        blk.tensor(f_t)
        blk.vector(f_v)

    with nc.Block("ln5") as blk:
        def f_v(v):
            for cci in range(NCC):
                v.tensor_tensor(xscr[:, :], xloc[:, cci, :], rbb[:, :], AL.mult)
                v.drain()
                v.tensor_tensor(xscr[:, :], xscr[:, :], mbb[:, :], AL.subtract)
                v.drain()
                v.tensor_copy(xnh[:, cci, :], xscr[:, :])
                v.drain()
                v.tensor_tensor(xnl[:, cci, :], xscr[:, :], xnh[:, cci, :],
                                AL.subtract)
                v.drain()

        blk.vector(f_v)

    lnx.close()
    if _PHASE_LIMIT <= 2:
        ctx.close()
        return nc


    rx = ExitStack()
    xtile = [sb(f"xtile{i}", [128, 2048], cx=rx) for i in range(2)]
    sqrow = sb("sqrow", [1, 2, 2048], cx=rx)
    xbf = sb("xbf", [128, 2048], BF16, cx=rx)
    xsqb = sb("xsqb", [128, 2048], BF16, cx=rx)
    lg = sb("lg", [10, TG], BF16, cx=rx)
    tmr = sb("tmr", [128, NCHG, 10], cx=rx)
    va = sb("va", [128, NCHG], cx=rx)
    vb = sb("vb", [128, NCHG], cx=rx)
    vv = sb("vv", [128, NCHG, 8], cx=rx)
    w1 = sb("w1", [128, NCHG, 8], cx=rx)
    w2 = sb("w2", [128, NCHG, 8], cx=rx)
    w3 = sb("w3", [128, NCHG, 8], cx=rx)
    r1c = sb("r1c", [128, NCHG], cx=rx)
    r2c = sb("r2c", [128, NCHG], cx=rx)
    csrow = sb("csrow", [1, 8], cx=rx)
    wmrow = sb("wmrow", [1, 24], cx=rx)
    rp0 = ExitStack()
    lgps = ps("lgps", [9, 2048], cx=rp0)
    sqps = ps("sqps", [1, 2048], cx=rp0)

    NTB = TG // 2048
    with nc.Block("r0") as blk:
        ld_b = LD.n
        se_b = SE.n
        te_b = TE.n
        ve_b = VE.n

        xts_b = XTS.i

        def f_sy(sy):
            for tb in range(NTB):
                for cci in range(NCC):
                    i = tb * NCC + cci
                    if i >= 2:
                        sy.wait_ge(SE.h, se_b + 2 * (i - 1))
                    XTS.issue(sy, xtile[i % 2][:, :],
                              bass.AP(xT.tensor, cci * 128 * TG + tb * 2048,
                                      [[TG, 128], [1, 2048]]))

        def f_s(s):
            for tb in range(NTB):
                for cci in range(NCC):
                    i = tb * NCC + cci
                    XTS.wait(s, xts_b + i)
                    if i >= 1:
                        s.wait_ge(TE.h, te_b + 8 * i)
                    s.copy(xbf[:, :], xtile[i % 2][:, :]).then_inc(SE.h, 1)
                    SE.n += 1
                    s.drain()
                    s.activation(xsqb[:, :], xbf[:, :], AF.Square).then_inc(SE.h, 1)
                    SE.n += 1

        def f_t(t):
            for tb in range(NTB):
                if tb >= 1:
                    t.wait_ge(VE.h, ve_b + 2 * tb)
                for cci in range(NCC):
                    i = tb * NCC + cci
                    t.wait_ge(SE.h, se_b + 2 * (i + 1))
                    for mbi in range(4):
                        sl = slice(mbi * 512, mbi * 512 + 512)
                        t.matmul(lgps[:, sl], gstat[:, cci, :], xbf[:, sl],
                                 start=(cci == 0), stop=(cci == NCC - 1)
                                 ).then_inc(TE.h, 1)
                        TE.n += 1
                        t.matmul(sqps[:, sl], ones_colb[:, :], xsqb[:, sl],
                                 start=(cci == 0), stop=(cci == NCC - 1)
                                 ).then_inc(TE.h, 1)
                        TE.n += 1

        sqs_b = SQS.i

        def f_v(v):
            for tb in range(NTB):
                v.wait_ge(TE.h, te_b + 64 * (tb + 1))
                if tb >= 2:
                    SQS.wait(v, sqs_b + tb - 2)
                v.tensor_copy(lg[0:9, tb * 2048:tb * 2048 + 2048], lgps[:, :])
                v.tensor_copy(sqrow[0:1, tb % 2, :], sqps[:, :]
                              ).then_inc(VE.h, 2)
                VE.n += 2

        def f_g(g):
            for tb in range(NTB):
                g.wait_ge(VE.h, ve_b + 2 * (tb + 1))
                SQS.issue(g, lg[9:10, tb * 2048:tb * 2048 + 2048],
                          sqrow[0:1, tb % 2, :])

        blk.sync(f_sy)
        blk.scalar(f_s)
        blk.tensor(f_t)
        blk.vector(f_v)
        blk.gpsimd(f_g)

    if _PHASE_LIMIT <= 21:
        rx.close()
        ctx.close()
        return nc

    rp0.close()
    trps2 = [ps(f"trps{i}", [128, 1024], cx=rx, dt=BF16) for i in range(2)]
    with nc.Block("r1") as blk:
        te_b = TE.n
        ve_b = VE.n

        def f_t(t):
            SQS.wait_all(t)
            for grp in range(16):
                if grp >= 2:
                    t.wait_ge(VE.h, ve_b + grp - 1)
                for j in range(8):
                    ch = grp * 8 + j
                    t.matmul(trps2[grp % 2][0:128, j * 10:j * 10 + 10],
                             lg[:, ch * 128:ch * 128 + 128],
                             idb16[0:10, 0:10], is_transpose=True
                             ).then_inc(TE.h, 1)
                    TE.n += 1

        def f_v(v):
            for grp in range(16):
                v.wait_ge(TE.h, te_b + 8 * (grp + 1))
                v.tensor_copy(tmr[:, grp * 8:grp * 8 + 8, :].rearrange(
                    "p a b -> p (a b)"), trps2[grp % 2][:, 0:80]).then_inc(VE.h, 1)
                VE.n += 1

        blk.tensor(f_t)
        blk.vector(f_v)

    if _PHASE_LIMIT <= 22:
        rx.close()
        ctx.close()
        return nc

    s8ps = ps("s8ps", [128, 512], cx=rx)
    with nc.Block("r2") as blk:
        def f_t(t):
            for cci in range(NCC):
                t.matmul(s8ps[0:1, 0:8], ones_col[:, :], gw32[:, cci, :],
                         start=(cci == 0), stop=(cci == NCC - 1)).then_inc(TE.h, 1)
                TE.n += 1

        def f_v(v):
            v.wait_ge(TE.h, TE.n)
            v.tensor_copy(csrow[0:1, :], s8ps[0:1, 0:8]).then_inc(VE.h, 1)
            VE.n += 1

        blk.tensor(f_t)
        blk.vector(f_v)

    with nc.Block("r3") as blk:
        def f_t(t):
            t.wait_ge(VE.h, VE.n)
            t.matmul(s8ps[:, 0:8], ones_row[:, :], csrow[0:1, :],
                     start=True, stop=True)
            t.matmul(s8ps[:, 8:16], ones_row[:, :], b8[0:1, :],
                     start=True, stop=True).then_inc(TE.h, 1)
            TE.n += 1

        def f_v(v):
            v.wait_ge(TE.h, TE.n)
            v.tensor_copy(cs8[:, :], s8ps[:, 0:8])
            v.tensor_copy(b8[:, :], s8ps[:, 8:16])

        blk.tensor(f_t)
        blk.vector(f_v)

    def tmsl(k):
        return bass.AP(tmr.tensor, k, [[NCHG * 10, 128], [10, NCHG], [1, 1]])

    def bcol(ap_t):
        return bass.AP(ap_t.tensor, ap_t.offset, [[NCHG, 128], [1, NCHG], [0, 8]])

    e8b = bass.AP(e8t.tensor, 0, [[8, 128], [0, NCHG], [1, 8]])
    r8b = bass.AP(r8t.tensor, 0, [[8, 128], [0, NCHG], [1, 8]])
    b8b = bass.AP(b8.tensor, 0, [[8, 128], [0, NCHG], [1, 8]])
    cs8b = bass.AP(cs8.tensor, 0, [[8, 128], [0, NCHG], [1, 8]])

    with nc.Block("r4") as blk:
        def f_v(v):
            v.tensor_scalar(va[:, :].rearrange("p (a b) -> p a b", b=1),
                            tmsl(8), INVC, None, AL.mult)
            v.drain()
            v.tensor_tensor(vb[:, :], va[:, :], va[:, :], AL.mult)
            v.tensor_scalar(r1c[:, :].rearrange("p (a b) -> p a b", b=1),
                            tmsl(9), INVC, None, AL.mult)
            v.drain()
            v.tensor_tensor(r1c[:, :], r1c[:, :], vb[:, :], AL.subtract
                            ).then_inc(VE.h, 1)
            VE.n += 1

        def f_s(s):
            s.wait_ge(VE.h, VE.n)
            s.activation(r1c[:, :], r1c[:, :], AF.Sqrt, bias=LN_EPS
                         ).then_inc(SE.h, 1)
            SE.n += 1

        blk.vector(f_v)
        blk.scalar(f_s)

    with nc.Block("r5") as blk:
        def f_v(v):
            v.wait_ge(SE.h, SE.n)
            v.reciprocal(r2c[:, :], r1c[:, :])
            v.drain()
            v.tensor_tensor(vb[:, :], va[:, :], r2c[:, :], AL.mult)
            v.drain()
            v.tensor_copy(va[:, :], r2c[:, :])
            v.drain()
            v.tensor_tensor(w1[:, :, :], tmr[:, :, 0:8], bcol(va), AL.mult)
            v.drain()
            v.tensor_tensor(w2[:, :, :], bcol(vb), cs8b, AL.mult)
            v.drain()
            v.tensor_tensor(w1[:, :, :], w1[:, :, :], w2[:, :, :], AL.subtract)
            v.drain()
            v.tensor_tensor(w1[:, :, :], w1[:, :, :], b8b, AL.add).then_inc(VE.h, 1)
            VE.n += 1

        def f_s(s):
            s.wait_ge(VE.h, VE.n)
            s.activation(vv[:, :, :], w1[:, :, :], AF.Sigmoid).then_inc(SE.h, 1)
            SE.n += 1

        blk.vector(f_v)
        blk.scalar(f_s)

    with nc.Block("r6") as blk:
        def f_v(v):
            v.wait_ge(SE.h, SE.n)
            v.reduce_max(r1c[:, :].rearrange("p (a b) -> p a b", b=1),
                         vv[:, :, :], axis=AX)
            v.drain()
            v.tensor_tensor(w1[:, :, :], vv[:, :, :], bcol(r1c), AL.subtract
                            ).then_inc(VE.h, 1)
            VE.n += 1

        def f_s(s):
            s.wait_ge(VE.h, VE.n)
            s.activation(w1[:, :, :], w1[:, :, :], AF.Relu, scale=HUGE,
                         bias=1.0).then_inc(SE.h, 1)
            SE.n += 1

        blk.vector(f_v)
        blk.scalar(f_s)

    with nc.Block("r7") as blk:
        def f_v(v):
            v.wait_ge(SE.h, SE.n)
            v.tensor_tensor(w2[:, :, :], w1[:, :, :], r8b, AL.mult)
            v.drain()
            v.tensor_scalar(w2[:, :, :], w2[:, :, :], -1.0, 8.0, AL.mult, AL.add)
            v.drain()
            v.tensor_reduce(r2c[:, :].rearrange("p (a b) -> p a b", b=1),
                            w2[:, :, :], AX, AL.min)
            v.drain()
            v.tensor_tensor(w2[:, :, :], e8b, bcol(r2c), AL.subtract
                            ).then_inc(VE.h, 1)
            VE.n += 1

        def f_s(s):
            s.wait_ge(VE.h, VE.n)
            s.activation(w2[:, :, :], w2[:, :, :], AF.Abs)
            s.drain()
            s.activation(w2[:, :, :], w2[:, :, :], AF.Relu, scale=-HUGE,
                         bias=1.0).then_inc(SE.h, 1)
            SE.n += 1

        blk.vector(f_v)
        blk.scalar(f_s)

    with nc.Block("r8") as blk:
        def f_v(v):
            v.wait_ge(SE.h, SE.n)
            v.scalar_tensor_tensor(w3[:, :, :], w2[:, :, :], -HUGE, vv[:, :, :],
                                   AL.mult, AL.add)
            v.drain()
            v.reduce_max(r2c[:, :].rearrange("p (a b) -> p a b", b=1),
                         w3[:, :, :], axis=AX)
            v.drain()
            v.tensor_tensor(w3[:, :, :], w3[:, :, :], bcol(r2c), AL.subtract
                            ).then_inc(VE.h, 1)
            VE.n += 1

        def f_s(s):
            s.wait_ge(VE.h, VE.n)
            s.activation(w3[:, :, :], w3[:, :, :], AF.Relu, scale=HUGE,
                         bias=1.0).then_inc(SE.h, 1)
            SE.n += 1

        blk.vector(f_v)
        blk.scalar(f_s)

    with nc.Block("r9") as blk:
        def f_v(v):
            v.wait_ge(SE.h, SE.n)
            v.tensor_tensor(r1c[:, :], r1c[:, :], r2c[:, :], AL.add)
            v.tensor_tensor(w3[:, :, :], w3[:, :, :], r8b, AL.mult)
            v.drain()
            v.tensor_scalar(w3[:, :, :], w3[:, :, :], -1.0, 8.0, AL.mult, AL.add)
            v.tensor_scalar(r1c[:, :], r1c[:, :], 1.0, 1e-8, AL.mult, AL.add)
            v.drain()
            v.tensor_reduce(r2c[:, :].rearrange("p (a b) -> p a b", b=1),
                            w3[:, :, :], AX, AL.min)
            v.reciprocal(r1c[:, :], r1c[:, :])
            v.drain()
            v.tensor_tensor(w3[:, :, :], e8b, bcol(r2c), AL.subtract
                            ).then_inc(VE.h, 1)
            VE.n += 1

        def f_s(s):
            s.wait_ge(VE.h, VE.n)
            s.activation(w3[:, :, :], w3[:, :, :], AF.Abs)
            s.drain()
            s.activation(w3[:, :, :], w3[:, :, :], AF.Relu, scale=-HUGE,
                         bias=1.0).then_inc(SE.h, 1)
            SE.n += 1

        blk.vector(f_v)
        blk.scalar(f_s)

    if _PHASE_LIMIT <= 23:
        rx.close()
        ctx.close()
        return nc

    smps = ps("smps", [1, 512], cx=rx)
    with nc.Block("r10") as blk:
        def f_v(v):
            v.wait_ge(SE.h, SE.n)
            v.tensor_tensor(w2[:, :, :], w2[:, :, :], w3[:, :, :], AL.add)
            v.drain()
            v.tensor_tensor(w1[:, :, :], w2[:, :, :], vv[:, :, :], AL.mult)
            v.drain()
            v.tensor_tensor(w1[:, :, :], w1[:, :, :], bcol(r1c), AL.mult)
            v.drain()
            v.tensor_reduce(w3[:, 0, :].rearrange("p (a b) -> p a b", b=1),
                            bass.AP(w1.tensor, 0,
                                    [[NCHG * 8, 128], [1, 8], [8, NCHG]]),
                            AX, AL.add)
            v.tensor_reduce(w3[:, 1, :].rearrange("p (a b) -> p a b", b=1),
                            bass.AP(w2.tensor, 0,
                                    [[NCHG * 8, 128], [1, 8], [8, NCHG]]),
                            AX, AL.add).then_inc(VE.h, 1)
            VE.n += 1

        def f_t(t):
            t.wait_ge(VE.h, VE.n)
            t.matmul(smps[0:1, 0:8], ones_col[:, :], w3[:, 0, :],
                     start=True, stop=True)
            t.matmul(smps[0:1, 8:16], ones_col[:, :], w3[:, 1, :],
                     start=True, stop=True).then_inc(TE.h, 1)
            TE.n += 1

        blk.vector(f_v)
        blk.tensor(f_t)

    with nc.Block("r11") as blk:
        def f_v(v):
            v.wait_ge(TE.h, TE.n)
            v.tensor_copy(wmrow[0:1, 0:16], smps[0:1, 0:16])
            v.drain()
            v.tensor_scalar(wmrow[0:1, 8:16], wmrow[0:1, 8:16], 1.0, None,
                            AL.max)
            v.drain()
            v.reciprocal(wmrow[0:1, 8:16], wmrow[0:1, 8:16])
            v.drain()
            v.tensor_tensor(wmrow[0:1, 16:24], wmrow[0:1, 0:8],
                            wmrow[0:1, 8:16], AL.mult).then_inc(VE.h, 1)
            VE.n += 1

        def f_t(t):
            t.wait_ge(VE.h, VE.n)
            t.matmul(s8ps[:, 0:8], ones_row[:, :], wmrow[0:1, 16:24],
                     start=True, stop=True).then_inc(TE.h, 1)
            TE.n += 1

        blk.vector(f_v)
        blk.tensor(f_t)

    with nc.Block("r12") as blk:
        def f_v(v):
            v.wait_ge(TE.h, TE.n)
            v.tensor_copy(wm8[:, :], s8ps[:, 0:8])
            v.drain()
            v.tensor_tensor(sc[:, :, :], w2[:, 0:NCHL, :],
                            bass.AP(wm8.tensor, 0, [[8, 128], [0, NCHL], [1, 8]]),
                            AL.mult)

        blk.vector(f_v)

    rx.close()
    if _PHASE_LIMIT <= 3:
        ctx.close()
        return nc


    fx = ExitStack()
    gsl = [sb("gsl0", [128, NCC, 128], cx=fx)] * 2
    usl = [sb("usl0", [128, NCC, 128], cx=fx)] * 2
    dsl = [sb("dsl0", [128, 4, C], cx=fx)] * 2
    gslb = [sb(f"gslb{i}", [128, 2, NCC, 128], BF16, cx=fx) for i in range(2)]
    uslb = [sb(f"uslb{i}", [128, 2, NCC, 128], BF16, cx=fx) for i in range(2)]
    dslb = [sb("dslb0", [128, 2, 4, C], BF16, cx=fx)] * 2
    auh = sb("auh", [128, 4, T], BF16, cx=fx)
    aul = sb("aul", [128, 4, T], BF16, cx=fx)
    a32 = sb("a32", [128, 512], cx=fx)
    sgb = [sb(f"sgb{i}", [128, 512], cx=fx) for i in range(2)]
    a1b = [sb("a1b0", [128, 512], cx=fx)] * 2
    scb = sb("scb", [128, T], cx=fx)
    sctp = sb("sctp", [16, 128], cx=fx)
    scrow = sb("scrow", [1, T], cx=fx)
    ybuf = sb("ybuf", [128, 4, 512], cx=fx)
    gps = [ps(f"gps{i}", [128, 512], cx=fx) for i in range(2)]
    ups = [ps(f"ups{i}", [128, 512], cx=fx) for i in range(2)]
    yps = [ps(f"yps{i}", [128, 512], cx=fx) for i in range(2)]
    scps = ps("scps", [128, 512], cx=fx)
    sctrp = ps("sctrp", [128, 512], cx=fx)

    NTT = 4

    def unit_wsrc(u):
        if u < E:
            return (egwT[u], euwT[u], edwT[u], IEXP)
        hh = u - E
        return (sgwT[:, hh * 512:hh * 512 + 512],
                suwT[:, hh * 512:hh * 512 + 512],
                sdwT[hh * 512:hh * 512 + 512, :], 2 * IEXP)

    prev_ve_at_split = None
    for u in range(10):
        if u >= int(os.environ.get("KUNITS", "10")):
            break
        gw_src, uw_src, dw_src, rs = unit_wsrc(u)
        has_sc = u < E

        if has_sc:
            with nc.Block(f"fsc{u}") as blk:
                def f_t(t, u=u):
                    t.matmul(sctrp[0:16, 0:128],
                             bass.AP(sc.tensor, u, [[NCHL * 8, 128], [8, NCHL]]),
                             id128[:, :], is_transpose=True).then_inc(TE.h, 1)
                    TE.n += 1

                def f_v(v):
                    v.wait_ge(TE.h, TE.n)
                    v.tensor_copy(sctp[:, :], sctrp[0:16, 0:128]).then_inc(VE.h, 1)
                    VE.n += 1

                blk.tensor(f_t)
                blk.vector(f_v)

            with nc.Block(f"fsr{u}") as blk:
                def f_sy(sy):
                    sy.wait_ge(VE.h, VE.n)
                    sy.dma_start(
                        bass.AP(scrow.tensor, 0, [[T, 1], [128, 16], [1, 128]]),
                        sctp[:, :]).then_inc(LD.h, 16)
                    LD.n += 16

                blk.sync(f_sy)

            with nc.Block(f"fsb{u}") as blk:
                te_b = TE.n
                ve_b = VE.n

                def f_t(t):
                    t.wait_ge(LD.h, LD.n)
                    for mbi in range(4):
                        if mbi >= 1:
                            t.wait_ge(VE.h, ve_b + mbi)
                        t.matmul(scps[:, :], ones_row[:, :],
                                 scrow[0:1, mbi * 512:mbi * 512 + 512],
                                 start=True, stop=True).then_inc(TE.h, 1)
                        TE.n += 1

                def f_v(v):
                    for mbi in range(4):
                        v.wait_ge(TE.h, te_b + mbi + 1)
                        v.tensor_copy(scb[:, mbi * 512:mbi * 512 + 512],
                                      scps[:, :]).then_inc(VE.h, 1)
                        VE.n += 1

                blk.tensor(f_t)
                blk.vector(f_v)

        with nc.Block(f"fgu{u}") as blk:
            ld_b = LD.n
            te_b = TE.n
            se_b = SE.n
            ve_b = VE.n

            wfs_b = WFS.i
            dds_tok_holder = []
            ve_at_split = {}
            ve_at_a = {}
            vc = ve_b + 1
            vc += 1
            ve_at_split[0] = vc
            for ic in range(4):
                for tt in range(NTT):
                    vc += 2
                    ve_at_a[ic * NTT + tt] = vc
                    if tt == 0 and ic < 3:
                        vc += 1
                        ve_at_split[ic + 1] = vc
            ve_end = vc

            def f_sy(sy, gw_src=gw_src, uw_src=uw_src, dw_src=dw_src, u=u,
                     rs=rs, pvs=prev_ve_at_split):
                dds_tok_holder.append(
                    DDS.issue(sy, dsl[u % 2][:, :, :],
                              bass.AP(dw_src.tensor, dw_src.offset,
                                      [[C, 128], [128 * C, 4], [1, C]])))
                for ic in range(4):
                    if ic >= 1:
                        sy.wait_ge(VE.h, ve_at_split[ic - 1])
                    elif pvs is not None:
                        sy.wait_ge(VE.h, pvs[3])
                    WFS.issue(sy, gsl[ic % 2][:, :, :],
                              bass.AP(gw_src.tensor, gw_src.offset + ic * 128,
                                      [[rs, 128], [128 * rs, NCC], [1, 128]]))
                    WFS.issue(sy, usl[ic % 2][:, :, :],
                              bass.AP(uw_src.tensor, uw_src.offset + ic * 128,
                                      [[rs, 128], [128 * rs, NCC], [1, 128]]))

            def f_v(v, has_sc=has_sc, te_b=te_b, se_b=se_b, u=u,
                    wfs_b=wfs_b):
                DDS.wait(v, dds_tok_holder[0])
                v.tensor_copy(dslb[u % 2][:, 0, :, :], dsl[u % 2][:, :, :])
                v.drain()
                v.tensor_tensor(dslb[u % 2][:, 1, :, :], dsl[u % 2][:, :, :],
                                dslb[u % 2][:, 0, :, :], AL.subtract
                                ).then_inc(VE.h, 1)
                VE.n += 1

                def split(ic):
                    WFS.wait(v, wfs_b + 2 * ic)
                    WFS.wait(v, wfs_b + 2 * ic + 1)
                    if ic >= 2:
                        v.wait_ge(TE.h, te_b + 192 * (ic - 1))
                    v.tensor_copy(gslb[ic % 2][:, 0, :, :], gsl[ic % 2][:, :, :])
                    v.tensor_copy(uslb[ic % 2][:, 0, :, :], usl[ic % 2][:, :, :])
                    v.drain()
                    v.tensor_tensor(gslb[ic % 2][:, 1, :, :],
                                    gsl[ic % 2][:, :, :],
                                    gslb[ic % 2][:, 0, :, :], AL.subtract)
                    v.tensor_tensor(uslb[ic % 2][:, 1, :, :],
                                    usl[ic % 2][:, :, :],
                                    uslb[ic % 2][:, 0, :, :], AL.subtract
                                    ).then_inc(VE.h, 1)
                    VE.n += 1
                    assert VE.n == ve_at_split[ic]

                split(0)
                for ic in range(4):
                    for tt in range(NTT):
                        it = ic * NTT + tt
                        sl = slice(tt * 512, tt * 512 + 512)
                        v.wait_ge(TE.h, te_b + 48 * (it + 1))
                        v.wait_ge(SE.h, se_b + it + 1)
                        v.drain()
                        if has_sc:
                            v.tensor_tensor(a1b[it % 2][:, :], ups[it % 2][:, :],
                                            scb[:, sl], AL.mult)
                            v.drain()
                            v.tensor_tensor(a32[:, :], a1b[it % 2][:, :],
                                            sgb[it % 2][:, :], AL.mult)
                        else:
                            v.tensor_tensor(a32[:, :], ups[it % 2][:, :],
                                            sgb[it % 2][:, :], AL.mult)
                        v.drain()
                        v.tensor_copy(auh[:, ic, sl], a32[:, :])
                        v.drain()
                        v.tensor_tensor(aul[:, ic, sl], a32[:, :],
                                        auh[:, ic, sl], AL.subtract
                                        ).then_inc(VE.h, 2)
                        VE.n += 2
                        assert VE.n == ve_at_a[it]
                        if tt == 0 and ic < 3:
                            split(ic + 1)

            def f_t(t, te_b=te_b, ve_b=ve_b):
                for ic in range(4):
                    t.wait_ge(VE.h, ve_at_split[ic])
                    for tt in range(NTT):
                        it = ic * NTT + tt
                        sl = slice(tt * 512, tt * 512 + 512)
                        if it >= 2:
                            t.wait_ge(VE.h, ve_at_a[it - 2])
                        for cci in range(NCC):
                            t.matmul(gps[it % 2][:, :],
                                     gslb[ic % 2][:, 0, cci, :],
                                     xnh[:, cci, sl], start=(cci == 0),
                                     stop=False).then_inc(TE.h, 1)
                            TE.n += 1
                            t.matmul(gps[it % 2][:, :],
                                     gslb[ic % 2][:, 0, cci, :],
                                     xnl[:, cci, sl], start=False,
                                     stop=False).then_inc(TE.h, 1)
                            TE.n += 1
                            t.matmul(gps[it % 2][:, :],
                                     gslb[ic % 2][:, 1, cci, :],
                                     xnh[:, cci, sl], start=False,
                                     stop=(cci == NCC - 1)).then_inc(TE.h, 1)
                            TE.n += 1
                        for cci in range(NCC):
                            t.matmul(ups[it % 2][:, :],
                                     uslb[ic % 2][:, 0, cci, :],
                                     xnh[:, cci, sl], start=(cci == 0),
                                     stop=False).then_inc(TE.h, 1)
                            TE.n += 1
                            t.matmul(ups[it % 2][:, :],
                                     uslb[ic % 2][:, 0, cci, :],
                                     xnl[:, cci, sl], start=False,
                                     stop=False).then_inc(TE.h, 1)
                            TE.n += 1
                            t.matmul(ups[it % 2][:, :],
                                     uslb[ic % 2][:, 1, cci, :],
                                     xnh[:, cci, sl], start=False,
                                     stop=(cci == NCC - 1)).then_inc(TE.h, 1)
                            TE.n += 1

            def f_s(s, te_b=te_b, ve_b=ve_b):
                for it in range(16):
                    s.wait_ge(TE.h, te_b + 48 * it + 24)
                    if it >= 2:
                        s.wait_ge(VE.h, ve_at_a[it - 2])
                    s.activation(sgb[it % 2][:, :], gps[it % 2][:, :],
                                 AF.Silu).then_inc(SE.h, 1)
                    SE.n += 1

            blk.sync(f_sy)
            blk.vector(f_v)
            blk.tensor(f_t)
            blk.scalar(f_s)

        prev_ve_at_split = ve_at_split

        with nc.Block(f"fdn{u}") as blk:
            te_b = TE.n
            se_b = SE.n
            ads_b = ADS.i

            def f_t(t, u=u, te_b=te_b, se_b=se_b):
                for tt in range(NTT):
                    for co in range(NCC):
                        j = tt * NCC + co
                        sl = slice(tt * 512, tt * 512 + 512)
                        if j >= 2:
                            t.wait_ge(SE.h, se_b + j - 1)
                        for ic in range(4):
                            dh = dslb[u % 2][:, 0, ic, co * 128:co * 128 + 128]
                            dl = dslb[u % 2][:, 1, ic, co * 128:co * 128 + 128]
                            t.matmul(yps[j % 2][:, :], dh, auh[:, ic, sl],
                                     start=(ic == 0), stop=False
                                     ).then_inc(TE.h, 1)
                            TE.n += 1
                            t.matmul(yps[j % 2][:, :], dh, aul[:, ic, sl],
                                     start=False, stop=False).then_inc(TE.h, 1)
                            TE.n += 1
                            t.matmul(yps[j % 2][:, :], dl, auh[:, ic, sl],
                                     start=False, stop=(ic == 3)
                                     ).then_inc(TE.h, 1)
                            TE.n += 1

            def f_s(s, te_b=te_b, ads_b=ads_b):
                for j in range(32):
                    s.wait_ge(TE.h, te_b + 12 * (j + 1))
                    if ads_b + j >= 4:
                        ADS.wait(s, ads_b + j - 4)
                    s.copy(ybuf[:, (ads_b + j) % 4, :], yps[j % 2][:, :]
                           ).then_inc(SE.h, 1)
                    SE.n += 1

            def f_g(g, se_b=se_b, ads_b=ads_b):
                ADS.wait_all(g)
                for tt in range(NTT):
                    for co in range(NCC):
                        j = tt * NCC + co
                        g.wait_ge(SE.h, se_b + j + 1)
                        if os.environ.get("KSUB") == "3":
                            tok = ADS.issue(g, out[co * 128:co * 128 + 128,
                                                   tt * 512:tt * 512 + 512],
                                            ybuf[:, (ads_b + j) % 4, :])
                        else:
                            tok = ADS.issue(g, out[co * 128:co * 128 + 128,
                                                   tt * 512:tt * 512 + 512],
                                            ybuf[:, (ads_b + j) % 4, :],
                                            accum_op=AL.add)
                        assert tok == ads_b + j

            blk.tensor(f_t)
            blk.scalar(f_s)
            blk.gpsimd(f_g)

    if _PHASE_LIMIT == 31:
        with nc.Block("dbg31") as blk:
            def f_sy(sy):
                sy.wait_ge(VE.h, VE.n)
                sy.wait_ge(SE.h, SE.n)
                sy.wait_ge(TE.h, TE.n)
                kdt = os.environ.get("KDT", "auh")
                if kdt == "xnh":
                    sy.dma_start(dbg[:, :], xnh[:, 0, :]).then_inc(LD.h, 16)
                elif kdt == "aul":
                    sy.dma_start(dbg[:, :], aul[:, 0, :]).then_inc(LD.h, 16)
                elif kdt == "scb":
                    sy.dma_start(dbg[:, 0:1024].rearrange("p (a b) -> p a b", a=1),
                                 bass.AP(scb.tensor, 0, [[2048, 128], [2, 1024]])
                                 ).then_inc(LD.h, 16)
                elif kdt == "gslb":
                    sy.dma_start(dbg[:, 0:1024],
                                 gslb[0][:, 0, :, :].rearrange("p a b -> p (a b)")
                                 ).then_inc(LD.h, 16)
                elif kdt == "ybuf":
                    sy.dma_start(dbg[:, 0:2048],
                                 ybuf[:, :, :].rearrange("p a b -> p (a b)")
                                 ).then_inc(LD.h, 16)
                else:
                    sy.dma_start(dbg[:, :], auh[:, 0, :]).then_inc(LD.h, 16)
                LD.n += 16
                sy.wait_ge(LD.h, LD.n)

            blk.sync(f_sy)
    fx.close()
    if _PHASE_LIMIT in (31, 4) and os.environ.get("KSUB") != "4":
        with nc.Block("fin31") as blk:
            def f_g(g):
                ADS.wait_all(g)

            blk.gpsimd(f_g)
    if _PHASE_LIMIT <= 4 or _PHASE_LIMIT == 31:
        ctx.close()
        return nc


    ax = ExitStack()
    wsl = [sb("wsl0", [128, NCC, 256], cx=ax)] * 2
    wslb = [sb(f"wslb{i}", [128, 2, NCC, 256], BF16, cx=ax) for i in range(2)]
    qtm = sb("qtm", [128, 16, 256], cx=ax)
    kpkh = sb("kpkh", [128, 2, T], BF16, cx=ax)
    kpkl = sb("kpkl", [128, 2, T], BF16, cx=ax)
    qEh = sb("qEh", [128, 2, T], BF16, cx=ax)
    qEl = sb("qEl", [128, 2, T], BF16, cx=ax)
    qOh = sb("qOh", [128, 2, T], BF16, cx=ax)
    qOl = sb("qOl", [128, 2, T], BF16, cx=ax)
    vth = sb("vth", [128, 2, 16, 132], BF16, cx=ax)
    vtl = sb("vtl", [128, 2, 16, 132], BF16, cx=ax)
    rsc1 = sb("rsc1", [128, 4, 256], cx=ax)
    rsc2 = sb("rsc2", [128, 4, 256], cx=ax)
    rsc3 = sb("rsc3", [128, 4, 256], cx=ax)
    msq = sb("msq", [128, 16, 4], cx=ax)
    p32 = [sb(f"p32{i}", [128, 256], cx=ax) for i in range(4)]
    pbh = [sb(f"pbh{i}", [128, 256], BF16, cx=ax) for i in range(4)]
    pbl = [sb(f"pbl{i}", [128, 256], BF16, cx=ax) for i in range(4)]
    oc1 = sb("oc1", [128, 2, 128], cx=ax)
    oc2 = sb("oc2", [128, 2, 128], cx=ax)
    occ = sb("occ", [128, 16], cx=ax)
    olnb = sb("olnb", [128, 256], cx=ax)
    oth = sb("oth", [128, 2, 256], BF16, cx=ax)
    otl = sb("otl", [128, 2, 256], BF16, cx=ax)
    sps = [ps(f"sps{i}", [128, 512], cx=ax) for i in range(3)]
    avps = [ps(f"avps{i}", [128, 512], cx=ax) for i in range(4)]
    otps = ps("otps", [128, 512], cx=ax)
    prps = [sps[0], sps[1], sps[2]]

    with nc.Block("aqz") as blk:
        def f_v(v):
            ADS.wait_all(v)
            v.memset(qEh[64:128, :, :], 0.0)
            v.memset(qEl[64:128, :, :], 0.0)
            v.memset(qOh[0:64, :, :], 0.0)
            v.memset(qOl[0:64, :, :], 0.0)

        def f_sy(sy):
            ADS.wait_all(sy)

        blk.vector(f_v)
        blk.sync(f_sy)

    for hg in range(4):
        for ti, (wt, name) in enumerate([(wqT, "q"), (wkT, "k"), (wvT, "v")]):
            with nc.Block(f"apl{hg}{name}") as blk:
                was_tok = []

                def f_sy(sy, wt=wt, ti=ti, hg=hg):
                    sy.wait_ge(VE.h, VE.n)
                    was_tok.append(
                        WAS.issue(sy, wsl[ti % 2][:, :, :],
                                  bass.AP(wt.tensor, hg * 256,
                                          [[C, 128], [128 * C, NCC], [1, 256]])))

                blk.sync(f_sy)

            with nc.Block(f"apj{hg}{name}") as blk:
                te_b = TE.n
                se_b = SE.n
                ve_b = VE.n

                def f_vsp(v, ti=ti):
                    WAS.wait(v, was_tok[0])
                    v.tensor_copy(wslb[ti % 2][:, 0, :, :], wsl[ti % 2][:, :, :])
                    v.drain()
                    v.tensor_tensor(wslb[ti % 2][:, 1, :, :],
                                    wsl[ti % 2][:, :, :],
                                    wslb[ti % 2][:, 0, :, :], AL.subtract
                                    ).then_inc(VE.h, 1)
                    VE.n += 1

                def f_t(t, ti=ti, se_b=se_b, ve_b=ve_b, name=name):
                    t.wait_ge(VE.h, ve_b + 1)
                    for ts in range(16):
                        if ts >= 3:
                            t.wait_ge(SE.h, se_b + ts - 2)
                            if name == "v":
                                t.wait_ge(VE.h, ve_b + 1 + ts - 2)
                        for cci in range(NCC):
                            xh_c = xnh[:, cci, ts * 128:ts * 128 + 128]
                            xl_c = xnl[:, cci, ts * 128:ts * 128 + 128]
                            wh_c = wslb[ti % 2][:, 0, cci, :]
                            wl_c = wslb[ti % 2][:, 1, cci, :]
                            t.matmul(prps[ts % 3][:, 0:256], xh_c, wh_c,
                                     start=(cci == 0), stop=False
                                     ).then_inc(TE.h, 1)
                            TE.n += 1
                            t.matmul(prps[ts % 3][:, 0:256], xh_c, wl_c,
                                     start=False, stop=False).then_inc(TE.h, 1)
                            TE.n += 1
                            t.matmul(prps[ts % 3][:, 0:256], xl_c, wh_c,
                                     start=False, stop=(cci == NCC - 1)
                                     ).then_inc(TE.h, 1)
                            TE.n += 1

                def f_s(s, name=name, te_b=te_b):
                    for ts in range(16):
                        s.wait_ge(TE.h, te_b + 24 * (ts + 1))
                        if name in ("q", "k"):
                            s.copy(qtm[:, ts, :], prps[ts % 3][:, 0:256]
                                   ).then_inc(SE.h, 1)
                            SE.n += 1
                        else:
                            s.copy(vth[:, 0, ts, 0:128], prps[ts % 3][:, 0:128])
                            s.copy(vth[:, 1, ts, 0:128], prps[ts % 3][:, 128:256]
                                   ).then_inc(SE.h, 1)
                            SE.n += 1

                def f_vlo(v, name=name, te_b=te_b, se_b=se_b):
                    if name != "v":
                        return
                    for ts in range(16):
                        v.wait_ge(SE.h, se_b + ts + 1)
                        v.tensor_tensor(vtl[:, 0, ts, 0:128],
                                        prps[ts % 3][:, 0:128],
                                        vth[:, 0, ts, 0:128], AL.subtract)
                        v.tensor_tensor(vtl[:, 1, ts, 0:128],
                                        prps[ts % 3][:, 128:256],
                                        vth[:, 1, ts, 0:128], AL.subtract
                                        ).then_inc(VE.h, 1)
                        VE.n += 1

                blk.vector(f_vsp)
                blk.tensor(f_t)
                blk.scalar(f_s)
                blk.vector(f_vlo)

            if name == "v":
                with nc.Block(f"avo{hg}") as blk:
                    def f_v(v):
                        v.memset(vth[:, :, :, 128:129], 1.0)
                        v.memset(vth[:, :, :, 129:132], 0.0)
                        v.memset(vtl[:, :, :, 128:132], 0.0)

                    blk.vector(f_v)
                continue

            dsti = 0 if name == "q" else 1
            qmul = 0.125 if name == "q" else None

            with nc.Block(f"arp{hg}{name}") as blk:
                def f_v(v):
                    for g4 in range(4):
                        tsl = slice(g4 * 4, g4 * 4 + 4)
                        if g4 >= 1:
                            v.drain()
                        v.tensor_tensor(rsc1[:, :, :], qtm[:, tsl, :],
                                        qtm[:, tsl, :], AL.mult)
                        v.drain()
                        v.reduce_sum(msq[:, tsl, :].rearrange(
                            "p a (b c) -> p a b c", c=1),
                            rsc1[:, :, :].rearrange("p a (b c) -> p a b c", b=4),
                            axis=AX)
                    v.drain()
                    v.tensor_scalar(msq[:, :, :], msq[:, :, :], 1.0 / HD,
                                    RMS_EPS, AL.mult, AL.add).then_inc(VE.h, 1)
                    VE.n += 1

                def f_s(s):
                    s.wait_ge(VE.h, VE.n)
                    s.activation(msq[:, :, :], msq[:, :, :], AF.Ln)
                    s.drain()
                    s.activation(msq[:, :, :], msq[:, :, :], AF.Exp, scale=-0.5
                                 ).then_inc(SE.h, 1)
                    SE.n += 1

                blk.vector(f_v)
                blk.scalar(f_s)

            with nc.Block(f"arr{hg}{name}") as blk:
                arr_b = VE.n

                def f_vr(v, qmul=qmul):
                    v.wait_ge(SE.h, SE.n)
                    if qmul is not None:
                        v.tensor_scalar(msq[:, :, :], msq[:, :, :], qmul, None,
                                        AL.mult)
                        v.drain()
                    for g4 in range(4):
                        q4 = bass.AP(qtm.tensor, g4 * 1024,
                                     [[4096, 128], [1, 1024]])
                        r1f = bass.AP(rsc1.tensor, 0, [[1024, 128], [1, 1024]])
                        r2f = bass.AP(rsc2.tensor, 0, [[1024, 128], [1, 1024]])
                        r3f = bass.AP(rsc3.tensor, 0, [[1024, 128], [1, 1024]])
                        msq_b = bass.AP(msq.tensor, g4 * 16,
                                        [[64, 128], [1, 16], [0, HD]])
                        cc_b = bass.AP(cc_all.tensor, g4 * 4 * HD,
                                       [[16 * HD, 128], [HD, 4], [0, 4], [1, HD]])
                        ss_b = bass.AP(ss_all.tensor, g4 * 4 * HD,
                                       [[16 * HD, 128], [HD, 4], [0, 4], [1, HD]])
                        r1sw = bass.AP(rsc1.tensor, 32,
                                       [[1024, 128], [64, 16], [-32, 2], [1, 32]])
                        q4v = bass.AP(qtm.tensor, g4 * 1024,
                                      [[4096, 128], [64, 16], [1, 64]])
                        r1v = bass.AP(rsc1.tensor, 0,
                                      [[1024, 128], [64, 16], [1, 64]])
                        r2v = bass.AP(rsc2.tensor, 0,
                                      [[1024, 128], [64, 16], [1, 64]])
                        r3v = bass.AP(rsc3.tensor, 0,
                                      [[1024, 128], [64, 16], [1, 64]])
                        v.tensor_tensor(r1v, q4v, msq_b, AL.mult)
                        v.drain()
                        v.tensor_tensor(r2v, r1v, cc_b, AL.mult)
                        v.tensor_tensor(r3v, r1sw, ss_b, AL.mult)
                        v.drain()
                        v.tensor_tensor(q4, r2f, r3f, AL.add
                                        ).then_inc(VE.h, 1)
                        VE.n += 1
                        v.drain()

                te_b = TE.n
                ve_b = VE.n + 4

                def f_t(t, te_b=te_b, ve_b=ve_b, arr_b=arr_b):
                    for grp in range(8):
                        t.wait_ge(VE.h, arr_b + (2 * grp + 1) // 4 + 1)
                        if grp >= 1:
                            t.wait_ge(VE.h, ve_b + grp)
                        for j in range(4):
                            ts2 = grp * 2 + j // 2
                            dc = j % 2
                            t.matmul(otps[:, j * 128:j * 128 + 128],
                                     qtm[:, ts2, dc * 128:dc * 128 + 128],
                                     id128[:, :], is_transpose=True
                                     ).then_inc(TE.h, 1)
                            TE.n += 1

                def f_v(v, dsti=dsti, te_b=te_b):
                    for grp in range(8):
                        v.wait_ge(TE.h, te_b + 4 * (grp + 1))
                        for j in range(4):
                            ts2 = grp * 2 + j // 2
                            dc = j % 2
                            tsl = slice(ts2 * 128, ts2 * 128 + 128)
                            pslc = otps[:, j * 128:j * 128 + 128]
                            if dsti == 1:
                                v.tensor_copy(kpkh[:, dc, tsl], pslc)
                                v.drain()
                                v.tensor_tensor(kpkl[:, dc, tsl], pslc,
                                                kpkh[:, dc, tsl], AL.subtract)
                            else:
                                ph_ = otps[0:64, j * 128:j * 128 + 128]
                                po_ = otps[64:128, j * 128:j * 128 + 128]
                                v.tensor_copy(qEh[0:64, dc, tsl], ph_)
                                v.tensor_copy(qOh[64:128, dc, tsl], po_)
                                v.drain()
                                v.tensor_tensor(qEl[0:64, dc, tsl], ph_,
                                                qEh[0:64, dc, tsl], AL.subtract)
                                v.tensor_tensor(qOl[64:128, dc, tsl], po_,
                                                qOh[64:128, dc, tsl],
                                                AL.subtract)
                        v.engine_nop().then_inc(VE.h, 1)
                        VE.n += 1

                blk.vector(f_vr)
                blk.tensor(f_t)
                blk.vector(f_v)

        for hl in range(2):
            h = hg * 2 + hl

            qt_items = []
            for qt in range(8):
                items = [(cp, kc) for cp in range(2) for kc in range(2 * qt + 2)]
                qt_items.append(items)

            with nc.Block(f"asv{hg}h{hl}") as blk:
                te_b = TE.n
                se_b = SE.n
                ve_b = VE.n
                te_after_s = {}
                te_after_av = {}
                te_group = {}
                te_tr = {}
                se_exp = {}
                se_evac = {}
                se_otcp = {}
                ve_p = {}
                ve_evac = {}
                ve_stt2 = {}
                ve_otl = {}
                tcnt, scnt, vcnt = te_b, se_b, ve_b
                for qt in range(8):
                    n = len(qt_items[qt])
                    for i in range(n + 3):
                        if i < n:
                            tcnt += 3
                            te_after_s[(qt, i)] = tcnt
                        if i >= 3:
                            tcnt += 6
                            te_after_av[(qt, i - 3)] = tcnt
                    te_group[qt] = tcnt
                    if qt >= 1:
                        tcnt += 2
                        te_tr[qt - 1] = tcnt
                tcnt += 2
                te_tr[7] = tcnt

                def _tab_A(m, scnt, vcnt):
                    ev = {}
                    sv = {}
                    vcnt += 1; ev["rcp0"] = vcnt
                    vcnt += 1; ev["rcp1"] = vcnt
                    scnt += 1; sv["t10"] = scnt
                    scnt += 1; sv["t11"] = scnt
                    vcnt += 1; ev["stt0"] = vcnt
                    vcnt += 1; ev["stt1"] = vcnt
                    ve_evac[m] = ev
                    se_evac[m] = sv
                    ve_stt2[m] = ev["stt1"]
                    return scnt, vcnt

                def _tab_B(m, scnt, vcnt):
                    ev = ve_evac[m]
                    sv = se_evac[m]
                    for sub in range(2):
                        vcnt += 1; ev["lnst%d" % sub] = vcnt
                        scnt += 1; sv["scl%d" % sub] = scnt
                        vcnt += 1; ev["bias%d" % sub] = vcnt
                        scnt += 1; sv["oln%d" % sub] = scnt
                    return scnt, vcnt

                for qt in range(8):
                    n = len(qt_items[qt])
                    for i in range(min(3, n)):
                        scnt += 1; se_exp[(qt, i)] = scnt
                        vcnt += 1; ve_p[(qt, i)] = vcnt
                    if qt >= 1:
                        scnt, vcnt = _tab_A(qt - 1, scnt, vcnt)
                    for i in range(min(3, n), min(6, n)):
                        scnt += 1; se_exp[(qt, i)] = scnt
                        vcnt += 1; ve_p[(qt, i)] = vcnt
                    if qt >= 1:
                        scnt, vcnt = _tab_B(qt - 1, scnt, vcnt)
                    if qt >= 2:
                        scnt += 1; se_otcp[qt - 2] = scnt
                        vcnt += 1; ve_otl[qt - 2] = vcnt
                    for i in range(min(6, n), n):
                        scnt += 1; se_exp[(qt, i)] = scnt
                        vcnt += 1; ve_p[(qt, i)] = vcnt
                scnt, vcnt = _tab_A(7, scnt, vcnt)
                scnt, vcnt = _tab_B(7, scnt, vcnt)
                scnt += 1; se_otcp[6] = scnt
                scnt += 1; se_otcp[7] = scnt
                vcnt += 1; ve_otl[6] = vcnt
                vcnt += 1; ve_otl[7] = vcnt
                te_total, se_total, ve_total = tcnt, scnt, vcnt
                ots_b = OTS.i

                def f_t(t, hl=hl):
                    def tr(m):
                        t.wait_ge(SE.h, se_evac[m]["oln1"])
                        if m >= 1:
                            t.wait_ge(SE.h, se_otcp[m - 1])
                            t.wait_ge(VE.h, ve_otl[m - 1])
                        t.matmul(otps[:, 0:128], olnb[:, 0:128], id128[:, :],
                                 is_transpose=True).then_inc(TE.h, 1)
                        TE.n += 1
                        t.matmul(otps[:, 128:256], olnb[:, 128:256],
                                 id128[:, :], is_transpose=True
                                 ).then_inc(TE.h, 1)
                        TE.n += 1
                        assert TE.n == te_tr[m]

                    for qt in range(8):
                        items = qt_items[qt]
                        n = len(items)
                        for i in range(n + 3):
                            if i < n:
                                cp, kc = items[i]
                                hh = 2 * hl + cp
                                if i >= 3:
                                    t.wait_ge(SE.h, se_exp[(qt, i - 3)])
                                elif qt >= 1:
                                    t.wait_ge(SE.h, se_exp[(qt - 1,
                                                            len(qt_items[qt - 1])
                                                            - 3 + i)])
                                dcq = hh // 2
                                qh_ = qEh if (hh % 2) == 0 else qOh
                                ql_ = qEl if (hh % 2) == 0 else qOl
                                ksl = slice(kc * 128, kc * 128 + 128)
                                qsl = slice(qt * 256, qt * 256 + 256)
                                t.matmul(sps[i % 3][:, 0:256],
                                         kpkh[:, dcq, ksl], qh_[:, dcq, qsl],
                                         start=True, stop=False).then_inc(TE.h, 1)
                                TE.n += 1
                                t.matmul(sps[i % 3][:, 0:256],
                                         kpkh[:, dcq, ksl], ql_[:, dcq, qsl],
                                         start=False, stop=False).then_inc(TE.h, 1)
                                TE.n += 1
                                t.matmul(sps[i % 3][:, 0:256],
                                         kpkl[:, dcq, ksl], qh_[:, dcq, qsl],
                                         start=False, stop=True).then_inc(TE.h, 1)
                                TE.n += 1
                                assert TE.n == te_after_s[(qt, i)]
                            if i >= 3:
                                cp, kc = items[i - 3]
                                t.wait_ge(VE.h, ve_p[(qt, i - 3)])
                                if kc == 0 and qt >= 1:
                                    if cp == 0:
                                        t.wait_ge(VE.h,
                                                  ve_evac[qt - 1]["rcp1"])
                                        t.wait_ge(SE.h,
                                                  se_evac[qt - 1]["t11"])
                                    else:
                                        t.wait_ge(VE.h, ve_stt2[qt - 1])
                                for sub in range(2):
                                    ssl = slice(sub * 128, sub * 128 + 128)
                                    pav = avps[cp * 2 + sub]
                                    t.matmul(pav[:, 0:132],
                                             pbh[(i - 3) % 4][:, ssl],
                                             vth[:, hl, kc, 0:132],
                                             start=(kc == 0), stop=False
                                             ).then_inc(TE.h, 1)
                                    TE.n += 1
                                    t.matmul(pav[:, 0:132],
                                             pbh[(i - 3) % 4][:, ssl],
                                             vtl[:, hl, kc, 0:132],
                                             start=False, stop=False
                                             ).then_inc(TE.h, 1)
                                    TE.n += 1
                                    t.matmul(pav[:, 0:132],
                                             pbl[(i - 3) % 4][:, ssl],
                                             vth[:, hl, kc, 0:132],
                                             start=False,
                                             stop=(kc == 2 * qt + 1)
                                             ).then_inc(TE.h, 1)
                                    TE.n += 1
                        assert TE.n == te_group[qt]
                        if qt >= 1:
                            tr(qt - 1)
                    tr(7)

                def f_s(s, hl=hl):
                    def otcp(m):
                        s.wait_ge(TE.h, te_tr[m])
                        tok0 = ots_b + 2 * (m - 2)
                        if tok0 >= 0:
                            OTS.wait(s, tok0)
                            OTS.wait(s, tok0 + 1)
                        s.copy(oth[:, m % 2, :], otps[:, 0:256]
                               ).then_inc(SE.h, 1)
                        SE.n += 1
                        assert SE.n == se_otcp[m]

                    def esA(m):
                        ev = ve_evac[m]
                        sv = se_evac[m]
                        for sub in range(2):
                            ob = 8 * sub
                            s.wait_ge(VE.h, ev["rcp%d" % sub])
                            s.activation(oc1[:, sub, :], avps[sub][:, 0:128],
                                         AF.Identity,
                                         scale=occ[:, ob + 0:ob + 1]
                                         ).then_inc(SE.h, 1)
                            SE.n += 1
                            assert SE.n == sv["t1%d" % sub]

                    def esB(m):
                        ev = ve_evac[m]
                        sv = se_evac[m]
                        for sub in range(2):
                            ob = 8 * sub
                            s.wait_ge(VE.h, ev["lnst%d" % sub])
                            s.activation(occ[:, ob + 6:ob + 7],
                                         occ[:, ob + 5:ob + 6], AF.Ln)
                            s.drain()
                            s.activation(occ[:, ob + 6:ob + 7],
                                         occ[:, ob + 6:ob + 7], AF.Exp,
                                         scale=-0.5).then_inc(SE.h, 1)
                            SE.n += 1
                            assert SE.n == sv["scl%d" % sub]
                            s.wait_ge(VE.h, ev["bias%d" % sub])
                            if sub == 0 and m >= 1:
                                s.wait_ge(TE.h, te_tr[m - 1])
                            s.activation(olnb[:, sub * 128:sub * 128 + 128],
                                         oc2[:, sub, :], AF.Identity,
                                         scale=occ[:, ob + 6:ob + 7],
                                         bias=occ[:, ob + 7:ob + 8]
                                         ).then_inc(SE.h, 1)
                            SE.n += 1
                            assert SE.n == sv["oln%d" % sub]

                    def expi(qt, i):
                        cp, kc = qt_items[qt][i]
                        s.wait_ge(TE.h, te_after_s[(qt, i)])
                        if i >= 4:
                            s.wait_ge(TE.h, te_after_av[(qt, i - 4)])
                        elif qt >= 1:
                            s.wait_ge(TE.h,
                                      te_after_av[(qt - 1,
                                                   len(qt_items[qt - 1])
                                                   - 4 + i)])
                        if kc >= 2 * qt:
                            s.activation(p32[i % 4][:, :], sps[i % 3][:, 0:256],
                                         AF.Exp).then_inc(SE.h, 1)
                        else:
                            s.activation(p32[i % 4][:, :], sps[i % 3][:, 0:256],
                                         AF.Exp)
                            s.activation(pbh[i % 4][:, :], sps[i % 3][:, 0:256],
                                         AF.Exp).then_inc(SE.h, 1)
                        SE.n += 1
                        assert SE.n == se_exp[(qt, i)]

                    for qt in range(8):
                        n = len(qt_items[qt])
                        for i in range(min(3, n)):
                            expi(qt, i)
                        if qt >= 1:
                            esA(qt - 1)
                        for i in range(min(3, n), min(6, n)):
                            expi(qt, i)
                        if qt >= 1:
                            esB(qt - 1)
                        if qt >= 2:
                            otcp(qt - 2)
                        for i in range(min(6, n), n):
                            expi(qt, i)
                    esA(7)
                    esB(7)
                    otcp(6)
                    otcp(7)

                def f_v(v, hl=hl):
                    def otlf(m):
                        v.wait_ge(SE.h, se_otcp[m])
                        v.tensor_tensor(otl[:, m % 2, :], otps[:, 0:256],
                                        oth[:, m % 2, :], AL.subtract
                                        ).then_inc(VE.h, 1)
                        VE.n += 1
                        assert VE.n == ve_otl[m]

                    def evA(m):
                        ev = ve_evac[m]
                        sv = se_evac[m]
                        v.wait_ge(TE.h, te_group[m])
                        for sub in range(2):
                            ob = 8 * sub
                            v.reciprocal(occ[:, ob + 0:ob + 1],
                                         avps[sub][:, 128:129])
                            v.reciprocal(occ[:, ob + 1:ob + 2],
                                         avps[2 + sub][:, 128:129])
                            v.drain()
                            v.tensor_tensor(occ[:, ob + 1:ob + 2],
                                            occ[:, ob + 1:ob + 2], lamneg[:, :],
                                            AL.mult).then_inc(VE.h, 1)
                            VE.n += 1
                            assert VE.n == ev["rcp%d" % sub]
                        for sub in range(2):
                            ob = 8 * sub
                            v.wait_ge(SE.h, sv["t1%d" % sub])
                            v.scalar_tensor_tensor(oc2[:, sub, :],
                                                   avps[2 + sub][:, 0:128],
                                                   occ[:, ob + 1:ob + 2],
                                                   oc1[:, sub, :], AL.mult,
                                                   AL.add).then_inc(VE.h, 1)
                            VE.n += 1
                            assert VE.n == ev["stt%d" % sub]

                    def evB(m):
                        ev = ve_evac[m]
                        sv = se_evac[m]
                        for sub in range(2):
                            ob = 8 * sub
                            v.drain()
                            v.reduce_sum(occ[:, ob + 2:ob + 3],
                                         oc2[:, sub, :].rearrange(
                                             "p (a b) -> p a b", a=1), axis=AX)
                            v.tensor_tensor(oc1[:, sub, :], oc2[:, sub, :],
                                            oc2[:, sub, :], AL.mult)
                            v.drain()
                            v.reduce_sum(occ[:, ob + 3:ob + 4],
                                         oc1[:, sub, :].rearrange(
                                             "p (a b) -> p a b", a=1), axis=AX)
                            v.tensor_scalar(occ[:, ob + 2:ob + 3],
                                            occ[:, ob + 2:ob + 3], 1.0 / 128,
                                            None, AL.mult)
                            v.drain()
                            v.tensor_scalar(occ[:, ob + 3:ob + 4],
                                            occ[:, ob + 3:ob + 4], 1.0 / 128,
                                            None, AL.mult)
                            v.tensor_tensor(occ[:, ob + 4:ob + 5],
                                            occ[:, ob + 2:ob + 3],
                                            occ[:, ob + 2:ob + 3], AL.mult)
                            v.drain()
                            v.tensor_tensor(occ[:, ob + 5:ob + 6],
                                            occ[:, ob + 3:ob + 4],
                                            occ[:, ob + 4:ob + 5], AL.subtract)
                            v.drain()
                            v.tensor_scalar(occ[:, ob + 5:ob + 6],
                                            occ[:, ob + 5:ob + 6], 1.0, LN_EPS,
                                            AL.mult, AL.add).then_inc(VE.h, 1)
                            VE.n += 1
                            assert VE.n == ev["lnst%d" % sub]
                            v.wait_ge(SE.h, sv["scl%d" % sub])
                            v.tensor_scalar(occ[:, ob + 6:ob + 7],
                                            occ[:, ob + 6:ob + 7], ONE_MINUS_LI,
                                            None, AL.mult)
                            v.drain()
                            v.scalar_tensor_tensor(occ[:, ob + 7:ob + 8],
                                                   occ[:, ob + 2:ob + 3], -1.0,
                                                   occ[:, ob + 6:ob + 7],
                                                   AL.mult, AL.mult
                                                   ).then_inc(VE.h, 1)
                            VE.n += 1
                            assert VE.n == ev["bias%d" % sub]

                    def spliti(qt, i):
                        cp, kc = qt_items[qt][i]
                        v.wait_ge(SE.h, se_exp[(qt, i)])
                        if i >= 4:
                            v.wait_ge(TE.h, te_after_av[(qt, i - 4)])
                        elif qt >= 1:
                            v.wait_ge(TE.h,
                                      te_after_av[(qt - 1,
                                                   len(qt_items[qt - 1])
                                                   - 4 + i)])
                        if kc >= 2 * qt:
                            msk = m0 if kc == 2 * qt else m1
                            v.tensor_tensor(p32[i % 4][:, :],
                                            p32[i % 4][:, :],
                                            msk[:, :], AL.mult)
                            v.drain()
                            v.tensor_copy(pbh[i % 4][:, :], p32[i % 4][:, :])
                            v.drain()
                        v.tensor_tensor(pbl[i % 4][:, :], p32[i % 4][:, :],
                                        pbh[i % 4][:, :], AL.subtract
                                        ).then_inc(VE.h, 1)
                        VE.n += 1
                        assert VE.n == ve_p[(qt, i)]

                    for qt in range(8):
                        n = len(qt_items[qt])
                        for i in range(min(3, n)):
                            spliti(qt, i)
                        if qt >= 1:
                            evA(qt - 1)
                        for i in range(min(3, n), min(6, n)):
                            spliti(qt, i)
                        if qt >= 1:
                            evB(qt - 1)
                        if qt >= 2:
                            otlf(qt - 2)
                        for i in range(min(6, n), n):
                            spliti(qt, i)
                    evA(7)
                    evB(7)
                    otlf(6)
                    otlf(7)

                def f_sy(sy, h=h):
                    for m in range(8):
                        sy.wait_ge(SE.h, se_otcp[m])
                        sy.wait_ge(VE.h, ve_otl[m])
                        OTS.issue(sy, otd[h][0][:, m * 256:m * 256 + 256],
                                  oth[:, m % 2, :])
                        OTS.issue(sy, otd[h][1][:, m * 256:m * 256 + 256],
                                  otl[:, m % 2, :])

                blk.tensor(f_t)
                blk.scalar(f_s)
                blk.vector(f_v)
                blk.sync(f_sy)
                assert TE.n == te_total and SE.n == se_total and VE.n == ve_total

    ax.close()
    if _PHASE_LIMIT <= 5:
        ctx.close()
        return nc


    px = ExitStack()
    wpsb = sb("wpsb", [128, NCC, C], cx=px)
    wpsbb = sb("wpsbb", [128, 2, NCC, C], BF16, cx=px)
    ybuf = sb("ybufp", [128, 4, 512], cx=px)
    otb = [sb(f"otb{i}", [128, 2, NCC, 512], BF16, cx=px) for i in range(2)]
    ypps = [ps(f"ypps{i}", [128, 512], cx=px) for i in range(2)]

    with nc.Block("pw0") as blk:
        def f_sy(sy):
            OTS.wait_all(sy)
            sy.dma_start(wpsb[:, :, :],
                         bass.AP(wpT.tensor, 0, [[C, 128], [128 * C, NCC], [1, C]])
                         ).then_inc(LD.h, 16)
            LD.n += 16

        def f_v(v):
            ADS.wait_all(v)
            v.wait_ge(LD.h, LD.n)
            v.tensor_copy(wpsbb[:, 0, :, :], wpsb[:, :, :])
            v.drain()
            v.tensor_tensor(wpsbb[:, 1, :, :], wpsb[:, :, :],
                            wpsbb[:, 0, :, :], AL.subtract).then_inc(VE.h, 1)
            VE.n += 1

        blk.sync(f_sy)
        blk.vector(f_v)

    if _PHASE_LIMIT <= 6:
        px.close()
        ctx.close()
        return nc

    with nc.Block("pw1") as blk:
        te_b = TE.n
        se_b = SE.n
        ve_b = VE.n
        ads_b = ADS.i
        obs_b = OBS.i

        def f_sy(sy, te_b=te_b):
            for tt in range(4):
                if tt >= 2:
                    sy.wait_ge(TE.h, te_b + 192 * (tt - 1))
                for hl_ in range(2):
                    OBS.issue(sy, otb[tt % 2][:, hl_, :, :],
                              bass.AP(otd.tensor, hl_ * 128 * T + tt * 512,
                                      [[T, 128], [2 * 128 * T, NCC], [1, 512]]))

        def f_t(t, se_b=se_b, obs_b=obs_b, ve_b=ve_b):
            t.wait_ge(VE.h, ve_b)
            for tt in range(4):
                OBS.wait(t, obs_b + 2 * tt)
                OBS.wait(t, obs_b + 2 * tt + 1)
                for co in range(NCC):
                    j = tt * NCC + co
                    if j >= 2:
                        t.wait_ge(SE.h, se_b + j - 1)
                    for hh in range(NCC):
                        wh_ = wpsbb[:, 0, hh, co * 128:co * 128 + 128]
                        wl_ = wpsbb[:, 1, hh, co * 128:co * 128 + 128]
                        if os.environ.get("KSUB") == "2":
                            t.matmul(ypps[j % 2][:, :], wh_,
                                     otb[tt % 2][:, 0, hh, :], start=(hh == 0),
                                     stop=(hh == NCC - 1)).then_inc(TE.h, 3)
                            TE.n += 3
                            continue
                        t.matmul(ypps[j % 2][:, :], wh_,
                                 otb[tt % 2][:, 0, hh, :], start=(hh == 0),
                                 stop=False).then_inc(TE.h, 1)
                        TE.n += 1
                        t.matmul(ypps[j % 2][:, :], wh_,
                                 otb[tt % 2][:, 1, hh, :], start=False,
                                 stop=False).then_inc(TE.h, 1)
                        TE.n += 1
                        t.matmul(ypps[j % 2][:, :], wl_,
                                 otb[tt % 2][:, 0, hh, :], start=False,
                                 stop=(hh == NCC - 1)).then_inc(TE.h, 1)
                        TE.n += 1

        def f_s(s, te_b=te_b, ads_b=ads_b):
            for j in range(32):
                s.wait_ge(TE.h, te_b + 24 * (j + 1))
                if ads_b + j >= 4:
                    ADS.wait(s, ads_b + j - 4)
                s.copy(ybuf[:, (ads_b + j) % 4, :], ypps[j % 2][:, :]
                       ).then_inc(SE.h, 1)
                SE.n += 1

        def f_g(g, se_b=se_b, ads_b=ads_b):
            ADS.wait_all(g)
            for tt in range(4):
                for co in range(NCC):
                    j = tt * NCC + co
                    g.wait_ge(SE.h, se_b + j + 1)
                    if os.environ.get("KSUB") == "1":
                        ADS.issue(g, out[co * 128:co * 128 + 128,
                                         tt * 512:tt * 512 + 512],
                                  ybuf[:, (ads_b + j) % 4, :])
                    else:
                        ADS.issue(g, out[co * 128:co * 128 + 128,
                                         tt * 512:tt * 512 + 512],
                                  ybuf[:, (ads_b + j) % 4, :], accum_op=AL.add)

        blk.sync(f_sy)
        blk.tensor(f_t)
        blk.scalar(f_s)
        blk.gpsimd(f_g)

    px.close()

    if _PHASE_LIMIT <= 7:
        ctx.close()
        return nc

    with nc.Block("fin") as blk:
        def f_g(g):
            ADS.wait_all(g)

        def f_sy(sy):
            sy.wait_ge(LD.h, LD.n)
            for st in (XTS, WFS, DDS, WAS, OTS, OBS, SQS):
                st.wait_all(sy)

        blk.gpsimd(f_g)
        blk.sync(f_sy)

    ctx.close()
    return nc


_CACHE = {}


def _prep_inputs(inputs):
    x = np.asarray(inputs["L0_attn_x"], dtype=np.float32)
    w = {}
    w["wqT"] = np.ascontiguousarray(np.asarray(inputs["L0_attn_Wq"]).T)
    w["wkT"] = np.ascontiguousarray(np.asarray(inputs["L0_attn_Wk"]).T)
    w["wvT"] = np.ascontiguousarray(np.asarray(inputs["L0_attn_Wv"]).T)
    w["wpT"] = np.ascontiguousarray(np.asarray(inputs["L0_attn_Wproj"]).T)
    w["gwTd"] = np.ascontiguousarray(np.asarray(inputs["L0_ffn_gate_w"]).T)
    w["lbd"] = np.asarray(inputs["L0_ffn_lb_bias"]).reshape(1, E)
    w["egwT"] = np.ascontiguousarray(
        np.asarray(inputs["L0_ffn_expert_gate_w"]).transpose(0, 2, 1))
    w["euwT"] = np.ascontiguousarray(
        np.asarray(inputs["L0_ffn_expert_up_w"]).transpose(0, 2, 1))
    w["edwT"] = np.ascontiguousarray(
        np.asarray(inputs["L0_ffn_expert_down_w"]).transpose(0, 2, 1))
    w["sgwT"] = np.ascontiguousarray(np.asarray(inputs["L0_ffn_shared_gate_w"]).T)
    w["suwT"] = np.ascontiguousarray(np.asarray(inputs["L0_ffn_shared_up_w"]).T)
    w["sdwT"] = np.ascontiguousarray(np.asarray(inputs["L0_ffn_shared_down_w"]).T)
    inv_freq = 1.0 / 10000.0 ** (np.arange(0, HD, 2, dtype=np.float32) / HD)
    tpos = np.arange(T, dtype=np.float32)
    freqs = np.outer(tpos, inv_freq)
    cosf = np.cos(freqs)
    sinf = np.sin(freqs)
    ct = np.zeros((128, 16, 2, 32), np.float32)
    st = np.zeros((128, 16, 2, 32), np.float32)
    for half in range(2):
        ct[:, :, half, :] = cosf.reshape(16, 128, 32).transpose(1, 0, 2)
        sgn = 1.0 if half == 0 else -1.0
        st[:, :, half, :] = sgn * sinf.reshape(16, 128, 32).transpose(1, 0, 2)
    w["cosd"] = np.ascontiguousarray(ct.reshape(128, 1024))
    w["sind"] = np.ascontiguousarray(st.reshape(128, 1024))
    w["lqk_d"] = np.concatenate([
        np.asarray(inputs["L0_attn_lq1"]), np.asarray(inputs["L0_attn_lk1"]),
        np.asarray(inputs["L0_attn_lq2"]), np.asarray(inputs["L0_attn_lk2"]),
    ]).reshape(1, 4 * HD).astype(np.float32)

    in_maps = []
    for b in range(8):
        rolled = np.concatenate([x[b:], x[:b]], axis=0)
        xTb = np.ascontiguousarray(rolled.reshape(TG, C).T)
        m = {"xT": xTb}
        m.update(w)
        in_maps.append(m)
    return in_maps


def kernel(**inputs):
    if "nc" not in _CACHE:
        _CACHE["nc"] = build_program()
    nc = _CACHE["nc"]
    in_maps = _prep_inputs(inputs)
    res = run_bass_kernel_spmd(nc, in_maps, core_ids=list(range(8)))
    outp = np.stack([np.asarray(res.results[b]["out"]).T for b in range(8)])
    return outp.astype(np.float32)
